# version 03
# baseline (speedup 1.0000x reference)
import contextlib
import math

import numpy as np
import ml_dtypes

import concourse.bass as bass
import concourse.mybir as mybir
from concourse.bass_utils import run_bass_kernel_spmd

F32 = mybir.dt.float32
BF16 = mybir.dt.bfloat16
F16 = mybir.dt.float16
I32 = mybir.dt.int32
AF = mybir.ActivationFunctionType
OP = mybir.AluOpType
BF16NP = ml_dtypes.bfloat16

B, S, D = 2, 8192, 2048
NTOK = B * S
TPC = NTOK // 8
NCH = D // 128
HEADS, DH = 16, 128
LAT = 512
NEG = -1.0e30
LN_EPS = 1e-5
RMS_EPS = 1e-6
MOE_I = 1408
SH_I = 5632
N_EXP = 8
CORES = list(range(8))
RSQD = 1.0 / math.sqrt(DH)


def pack16(w):
    din, dout = w.shape
    cc, ob = din // 128, dout // 128
    return np.ascontiguousarray(
        w.reshape(cc, 128, ob, 128).transpose(2, 1, 0, 3)).reshape(
        ob, 128, cc * 128)


def pack_mov(w, nblk=512):
    din, dout = w.shape
    cc, ob = din // 128, dout // nblk
    return np.ascontiguousarray(
        w.reshape(cc, 128, ob, nblk).transpose(2, 1, 0, 3)).reshape(
        ob, 128, cc * nblk)


def causal_mask_tiles():
    k = np.arange(128)[:, None]
    q = np.arange(512)[None, :]
    return np.stack([np.where(q >= 128 * r + k, 0.0, NEG) for r in range(4)]
                    ).astype(np.float32)


class Sem:
    def __init__(self, prog, name):
        self.h = prog.es.enter_context(prog.nc.semaphore(name))
        self.name = name
        self.n = 0
        self.maxw = 0

    def inc(self, k=1):
        self.n += k
        h = self.h

        def _apply(instr):
            instr.then_inc(h, k)
        return _apply

    def wait(self, eng, v):
        assert v >= 0
        self.maxw = max(self.maxw, v)
        eng.wait_ge(self.h, v)


class Slots:

    def __init__(self, n):
        self.req = [None] * n

    def acquire(self, i):
        req = self.req[i]

        def _w(eng):
            if req is not None:
                req[0].wait(eng, req[1])
        return _w

    def release_at(self, i, sem):
        self.req[i] = (sem, sem.n)


class Prog:
    ENGS = ("tensor", "vector", "scalar", "gpsimd", "sync")

    def __init__(self, name):
        self.nc = bass.Bass("TRN2", target_bir_lowering=False, debug=False,
                            detect_race_conditions=False)
        self.es = contextlib.ExitStack()
        self.streams = {e: [] for e in self.ENGS}
        self.sems = []
        self._sn = 0

    def dram(self, name, shape, dt, kind="ExternalInput"):
        return self.nc.dram_tensor(name, shape, dt, kind=kind).ap()

    def sb(self, name, shape, dt):
        return self.es.enter_context(self.nc.sbuf_tensor(name, shape, dt))

    def ps(self, name, shape, dt=F32):
        return self.es.enter_context(self.nc.psum_tensor(name, shape, dt))

    def sem(self, name=None):
        s = Sem(self, name or f"s{self._sn}")
        self._sn += 1
        self.sems.append(s)
        return s

    def on(self, eng, fn):
        self.streams[eng].append(fn)

    def finish(self):
        with self.nc.Block() as blk:
            for e in self.ENGS:
                ops = self.streams[e]
                if not ops:
                    continue

                def body(eng, ops=ops):
                    for op in ops:
                        op(eng)
                getattr(blk, e)(body)
        for s in self.sems:
            assert s.maxw <= s.n, (s.name, "wait", s.maxw, "> produced", s.n)
        self.es.close()
        return self.nc


def build_launch_A():
    p = Prog("A")
    xT = p.dram("xT", [D, TPC], F32)
    wq = p.dram("wq", [16, 128, NCH * 128], F32)
    wdown = p.dram("wdown", [4, 128, NCH * 128], F32)
    wkup = p.dram("wkup", [16, 128, 4 * 128], F32)
    wvup = p.dram("wvup", [4, 128, 4 * 512], F32)
    gup = p.dram("gup", [11, 128, 256], I32)
    gusT = p.dram("gusT", [11, 128, 16], F16)
    dnp = p.dram("dnp", [2, 128, 704], I32)
    dnsT = p.dram("dnsT", [2, 128, 44], F16)
    ident = p.dram("ident", [128, 128], BF16)

    q0t = p.dram("q0t", [D, TPC], BF16, "ExternalOutput")
    k0t = p.dram("k0t", [D, TPC], BF16, "ExternalOutput")
    v0 = p.dram("v0", [TPC, D], BF16, "ExternalOutput")
    wguTs = p.dram("wguTs", [D, 11 * 128], BF16, "ExternalOutput")
    wdnTs = p.dram("wdnTs", [SH_I, 2 * 128], BF16, "ExternalOutput")

    TT = 512
    xf = p.sb("xf", [128, NCH, TT], F32)
    xb = p.sb("xb", [128, 2, TT], BF16)
    xq = p.sb("xq", [128, 2, TT], BF16)
    mrow = p.sb("mrow", [1, TT], F32)
    vrow = p.sb("vrow", [1, TT], F32)
    srow = p.sb("srow", [1, TT], F32)
    arow = p.sb("arow", [1, TT], F32)
    brow = p.sb("brow", [1, TT], F32)
    tmpf = p.sb("tmpf", [128, TT], F32)
    xhat = p.sb("xhat", [128, NCH, TPC], BF16)
    cT = p.sb("cT", [128, 4, TPC], BF16)
    ones_b = p.sb("ones_b", [128, 1], BF16)
    ones_r = p.sb("ones_r", [1, 128], F32)
    idn = p.sb("idn", [128, 128], BF16)
    wt = p.sb("wt", [128, 2, NCH * 128], BF16)
    wmv = p.sb("wmv", [128, 2, 4 * 512], BF16)
    obuf = p.sb("obuf", [128, 4, 512], BF16)

    ps_a = p.ps("ps_a", [128, 512])
    ps_b = p.ps("ps_b", [128, 512])
    ps_s1 = p.ps("ps_s1", [1, 512])
    ps_s2 = p.ps("ps_s2", [1, 512])
    ps_r1 = p.ps("ps_r1", [128, 512])
    ps_r2 = p.ps("ps_r2", [128, 512])
    ps_ab = [ps_a, ps_b]

    s_x = p.sem("s_x")
    s_cast = p.sem("s_cast")
    s_stmm = p.sem("s_stmm")
    s_vr = p.sem("s_vr")
    s_sq = p.sem("s_sq")
    s_row = p.sem("s_row")
    s_bc = p.sem("s_bc")
    s_ap = p.sem("s_ap")

    p.on("gpsimd", lambda g: g.memset(ones_b[:, :], 1.0))
    p.on("gpsimd", lambda g: g.memset(ones_r[:, :], 1.0))
    cb0 = s_x.inc(16)
    p.on("gpsimd", lambda g, cb=cb0: cb(g.dma_start(out=idn[:, :], in_=ident[:, :])))

    for tt in range(4):
        if tt > 0:
            p.on("gpsimd", lambda g, v=tt * NCH: s_ap.wait(g, v))
        xprev = s_x.n
        cb = s_x.inc(16)

        def xtile(g, tt=tt, cb=cb, xprev=xprev):
            s_x.wait(g, xprev)
            cb(g.dma_start(
                out=xf[:, :, :],
                in_=xT[:, tt * TT:(tt + 1) * TT].rearrange(
                    "(c p) t -> p c t", p=128)))
        p.on("gpsimd", xtile)
        xtot = s_x.n
        for f in range(NCH):
            i = tt * NCH + f
            cb = s_cast.inc(1)

            def cast_op(sc, f=f, i=i, cb=cb, xtot=xtot):
                s_x.wait(sc, xtot)
                if i >= 2:
                    s_stmm.wait(sc, i - 1)
                sc.copy(xb[:, f % 2, :], xf[:, f, :])
                cb(sc.activation(xq[:, f % 2, :], xf[:, f, :], AF.Square))
            p.on("scalar", cast_op)

            cb2 = s_stmm.inc(1)

            def st_mm(t, f=f, i=i, tt=tt, cb2=cb2):
                s_cast.wait(t, i + 1)
                if f == 0 and tt > 0:
                    s_row.wait(t, tt)
                t.matmul(ps_s1[:, :], ones_b[:, :], xb[:, f % 2, :],
                         start=(f == 0), stop=(f == NCH - 1))
                cb2(t.matmul(ps_s2[:, :], ones_b[:, :], xq[:, f % 2, :],
                             start=(f == 0), stop=(f == NCH - 1)))
            p.on("tensor", st_mm)

        cbv = s_vr.inc(1)

        def rows_a(v, tt=tt, cbv=cbv):
            s_stmm.wait(v, (tt + 1) * NCH)
            s_sq.wait(v, tt)
            v.tensor_scalar(mrow[:, :], ps_s1[:, :], 1.0 / D, None, OP.mult)
            v.tensor_tensor(srow[:, :], mrow[:, :], mrow[:, :], OP.mult)
            v.tensor_scalar(vrow[:, :], ps_s2[:, :], 1.0 / D, LN_EPS,
                            OP.mult, OP.add)
            cbv(v.tensor_tensor(vrow[:, :], vrow[:, :], srow[:, :], OP.subtract))
        p.on("vector", rows_a)

        cbs = s_sq.inc(1)

        def rows_b(sc, tt=tt, cbs=cbs):
            s_vr.wait(sc, tt + 1)
            s_bc.wait(sc, tt)
            cbs(sc.activation(arow[:, :], vrow[:, :], AF.Sqrt))
        p.on("scalar", rows_b)

        cbr = s_row.inc(1)

        def rows_c(v, tt=tt, cbr=cbr):
            s_sq.wait(v, tt + 1)
            v.reciprocal(arow[:, :], arow[:, :])
            v.drain()
            cbr(v.scalar_tensor_tensor(brow[:, :], mrow[:, :], -1.0,
                                       arow[:, :], OP.mult, OP.mult))
        p.on("vector", rows_c)

        cbb = s_bc.inc(1)

        def bc_op(t, tt=tt, cbb=cbb):
            s_row.wait(t, tt + 1)
            if tt > 0:
                s_ap.wait(t, tt * NCH)
            t.matmul(ps_r1[:, :], ones_r[:, :], arow[:, :], start=True, stop=True)
            cbb(t.matmul(ps_r2[:, :], ones_r[:, :], brow[:, :], start=True,
                         stop=True))
        p.on("tensor", bc_op)

        for f in range(NCH):
            cba = s_ap.inc(1)

            def apply_op(v, f=f, tt=tt, cba=cba):
                s_bc.wait(v, tt + 1)
                v.tensor_tensor(tmpf[:, :], xf[:, f, :], ps_r1[:, :], OP.mult)
                cba(v.tensor_tensor(xhat[:, f, tt * TT:(tt + 1) * TT],
                                    tmpf[:, :], ps_r2[:, :], OP.add))
            p.on("vector", apply_op)

    s_w = p.sem("s_w")
    s_pm = p.sem("s_pm")
    s_ev = p.sem("s_ev")
    s_od = p.sem("s_od")
    wt_slots = Slots(2)
    wmv_slots = Slots(2)
    bank_slots = Slots(2)
    ob_slots = Slots(4)
    st = {"pm": 0, "ev": 0, "w": 0, "mv": 0}

    def fm_proj(w_hbm, nob, ncc, rhs_sb, out_hbm=None, keep_sb=None,
                first_wait=None):
        for ob in range(nob):
            wslot = st["w"] % 2
            st["w"] += 1
            aw = wt_slots.acquire(wslot)
            cbw = s_w.inc(16)
            wneed = s_w.n

            def wload(g, aw=aw, cbw=cbw, wslot=wslot, w_hbm=w_hbm, ob=ob,
                      ncc=ncc, wprev=wneed - 16):
                aw(g)
                s_w.wait(g, wprev)
                cbw(g.dma_start(out=wt[:, wslot, 0:ncc * 128], in_=w_hbm[ob]))
            p.on("gpsimd", wload)

            for tt in range(4):
                gidx = st["pm"]
                bank = gidx % 2
                st["pm"] += 1
                awb = bank_slots.acquire(bank)
                cbm = s_pm.inc(1)

                def mmop(t, tt=tt, bank=bank, wslot=wslot, wneed=wneed,
                         awb=awb, cbm=cbm, fw=(first_wait if (ob == 0 and
                                                              tt == 0) else None),
                         rhs_sb=rhs_sb, ncc=ncc, ob=ob):
                    if fw is not None:
                        fw(t)
                    s_w.wait(t, wneed)
                    awb(t)
                    for cc2 in range(ncc):
                        mm = t.matmul(ps_ab[bank][:, :],
                                      wt[:, wslot, cc2 * 128:(cc2 + 1) * 128],
                                      rhs_sb[:, cc2, tt * 512:(tt + 1) * 512],
                                      start=(cc2 == 0), stop=(cc2 == ncc - 1))
                    cbm(mm)
                p.on("tensor", mmop)

                pmneed = s_pm.n
                if keep_sb is not None:
                    cbe = s_ev.inc(1)

                    def evop(v, ob=ob, tt=tt, bank=bank, pmneed=pmneed,
                             cbe=cbe, keep_sb=keep_sb):
                        s_pm.wait(v, pmneed)
                        cbe(v.tensor_copy(
                            keep_sb[:, ob, tt * 512:(tt + 1) * 512],
                            ps_ab[bank][:, :]))
                    p.on("vector", evop)
                    bank_slots.release_at(bank, s_ev)
                else:
                    oslot = st["ev"] % 4
                    awo = ob_slots.acquire(oslot)
                    cbe = s_ev.inc(1)

                    def evop(v, bank=bank, oslot=oslot, pmneed=pmneed,
                             awo=awo, cbe=cbe):
                        s_pm.wait(v, pmneed)
                        awo(v)
                        cbe(v.tensor_copy(obuf[:, oslot, :], ps_ab[bank][:, :]))
                    p.on("vector", evop)
                    bank_slots.release_at(bank, s_ev)
                    evneed = s_ev.n
                    cbo = s_od.inc(16)

                    def odop(g, ob=ob, tt=tt, oslot=oslot, evneed=evneed,
                             cbo=cbo, out_hbm=out_hbm, oprev=s_od.n - 16):
                        s_ev.wait(g, evneed)
                        s_od.wait(g, oprev)
                        cbo(g.dma_start(
                            out=out_hbm[ob * 128:(ob + 1) * 128,
                                        tt * 512:(tt + 1) * 512],
                            in_=obuf[:, oslot, :]))
                    p.on("sync", odop)
                    ob_slots.release_at(oslot, s_od)
                st["ev"] += 1
            wt_slots.release_at(wslot, s_pm)

    def first_wait_ap(t):
        s_ap.wait(t, 4 * NCH)

    fm_proj(wq, 16, NCH, xhat, out_hbm=q0t, first_wait=first_wait_ap)
    fm_proj(wdown, 4, NCH, xhat, keep_sb=cT)
    fm_proj(wkup, 16, 4, cT, out_hbm=k0t)

    for oc in range(4):
        mslot = st["mv"] % 2
        st["mv"] += 1
        aw = wmv_slots.acquire(mslot)
        cbw = s_w.inc(16)
        wneed = s_w.n

        def wvload(g, oc=oc, mslot=mslot, aw=aw, cbw=cbw, wprev=wneed - 16):
            aw(g)
            s_w.wait(g, wprev)
            cbw(g.dma_start(out=wmv[:, mslot, :], in_=wvup[oc]))
        p.on("gpsimd", wvload)

        for tk in range(16):
            gidx = st["pm"]
            bank = gidx % 2
            st["pm"] += 1
            awb = bank_slots.acquire(bank)
            cbm = s_pm.inc(1)

            def mmop(t, tk=tk, bank=bank, mslot=mslot, wneed=wneed, awb=awb,
                     cbm=cbm):
                s_w.wait(t, wneed)
                awb(t)
                for cc2 in range(4):
                    mm = t.matmul(ps_ab[bank][:, :],
                                  cT[:, cc2, tk * 128:(tk + 1) * 128],
                                  wmv[:, mslot, cc2 * 512:(cc2 + 1) * 512],
                                  start=(cc2 == 0), stop=(cc2 == 3))
                cbm(mm)
            p.on("tensor", mmop)

            pmneed = s_pm.n
            oslot = st["ev"] % 4
            st["ev"] += 1
            awo = ob_slots.acquire(oslot)
            cbe = s_ev.inc(1)

            def evop(v, bank=bank, oslot=oslot, pmneed=pmneed, awo=awo,
                     cbe=cbe):
                s_pm.wait(v, pmneed)
                awo(v)
                cbe(v.tensor_copy(obuf[:, oslot, :], ps_ab[bank][:, :]))
            p.on("vector", evop)
            bank_slots.release_at(bank, s_ev)
            evneed = s_ev.n
            cbo = s_od.inc(16)

            def odop(g, oc=oc, tk=tk, oslot=oslot, evneed=evneed, cbo=cbo,
                     oprev=s_od.n - 16):
                s_ev.wait(g, evneed)
                s_od.wait(g, oprev)
                cbo(g.dma_start(
                    out=v0[tk * 128:(tk + 1) * 128, oc * 512:(oc + 1) * 512],
                    in_=obuf[:, oslot, :]))
            p.on("sync", odop)
            ob_slots.release_at(oslot, s_od)
        wmv_slots.release_at(mslot, s_pm)

    DQC = 2048
    dq_w = p.sb("dq_w", [128, 2, 704], I32)
    dq_s = p.sb("dq_s", [128, 2, 44], F32)
    dq_n = p.sb("dq_n", [128, 256], I32)
    dq_m = p.sb("dq_m", [128, DQC], I32)
    dq_a = p.sb("dq_a", [128, DQC], I32)
    dq_r = p.sb("dq_r", [128, DQC], F32)
    dq_v = p.sb("dq_v", [128, DQC], F32)
    dq_f = p.sb("dq_f", [128, DQC], F32)
    dq_b = p.sb("dq_b", [128, DQC], BF16)
    dq_o = p.sb("dq_o", [128, 4, 128], BF16)
    ps_t = [p.ps("ps_t0", [128, 128], BF16), p.ps("ps_t1", [128, 128], BF16)]

    s_dw = p.sem("s_dw")
    s_dv = p.sem("s_dv")
    s_dt = p.sem("s_dt")
    s_de = p.sem("s_de")
    s_do = p.sem("s_do")
    dqw_slots = Slots(2)
    dqb_slots = Slots(1)
    pst_slots = Slots(2)
    dqo_slots = Slots(4)
    dq_idx = {"t": 0, "o": 0}

    def dequant_unit(uidx, packed_hbm, scaleT_hbm, pk0, npk, gr0, out_hbm,
                     oloc, in0):
        wslot = uidx % 2
        aw = dqw_slots.acquire(wslot)
        cb1 = s_dw.inc(16)
        cb2 = s_dw.inc(16)

        def dload(g, aw=aw, cb1=cb1, cb2=cb2, wslot=wslot,
                  packed_hbm=packed_hbm, scaleT_hbm=scaleT_hbm, pk0=pk0,
                  npk=npk, gr0=gr0, dprev=s_dw.n - 32):
            aw(g)
            s_dw.wait(g, dprev)
            cb1(g.dma_start(out=dq_w[:, wslot, 0:npk],
                            in_=packed_hbm[:, pk0:pk0 + npk]))
            s_dw.wait(g, dprev + 16)
            cb2(g.dma_start(out=dq_s[:, wslot, 0:npk // 16],
                            in_=scaleT_hbm[:, gr0:gr0 + npk // 16]))
        p.on("gpsimd", dload)
        dwn = s_dw.n
        nin = npk * 8
        ngr = npk // 16
        awb = dqb_slots.acquire(0)
        cbv = s_dv.inc(1)

        def dcomp(v, wslot=wslot, npk=npk, nin=nin, ngr=ngr, dwn=dwn, awb=awb,
                  cbv=cbv):
            s_dw.wait(v, dwn)
            awb(v)
            for j in range(8):
                v.tensor_scalar(dq_n[:, 0:npk], dq_w[:, wslot, 0:npk],
                                4 * j, 15, OP.logical_shift_right,
                                OP.bitwise_and)
                v.tensor_scalar(dq_m[:, j:nin:8], dq_n[:, 0:npk], 7, None,
                                OP.bitwise_and)
                v.tensor_scalar(dq_a[:, j:nin:8], dq_n[:, 0:npk], 8, None,
                                OP.bitwise_and)
            v.tensor_scalar(dq_f[:, 0:nin], dq_a[:, 0:nin], -0.25, 1.0,
                            OP.mult, OP.add)
            v.tensor_scalar(dq_r[:, 0:nin], dq_m[:, 0:nin], -4.0, 0.0,
                            OP.add, OP.max)
            v.tensor_scalar(dq_v[:, 0:nin], dq_m[:, 0:nin], -6.0, 0.0,
                            OP.add, OP.max)
            v.scalar_tensor_tensor(dq_r[:, 0:nin], dq_r[:, 0:nin], 0.5,
                                   dq_v[:, 0:nin], OP.mult, OP.add)
            v.scalar_tensor_tensor(dq_v[:, 0:nin], dq_m[:, 0:nin], 0.5,
                                   dq_r[:, 0:nin], OP.mult, OP.add)
            v.tensor_tensor(dq_v[:, 0:nin], dq_v[:, 0:nin], dq_f[:, 0:nin],
                            OP.mult)
            for g2 in range(ngr):
                cbx = cbv if g2 == ngr - 1 else (lambda x: x)
                cbx(v.tensor_scalar(dq_b[:, g2 * 128:(g2 + 1) * 128],
                                    dq_v[:, g2 * 128:(g2 + 1) * 128],
                                    dq_s[:, wslot, g2:g2 + 1],
                                    None, OP.mult))
        p.on("vector", dcomp)
        dqw_slots.release_at(wslot, s_dv)
        dvn = s_dv.n

        for i2 in range(nin // 128):
            tb = dq_idx["t"] % 2
            dq_idx["t"] += 1
            awt = pst_slots.acquire(tb)
            cbt = s_dt.inc(1)

            def ttr(t, i2=i2, tb=tb, dvn=dvn, awt=awt, cbt=cbt):
                s_dv.wait(t, dvn)
                awt(t)
                cbt(t.transpose(ps_t[tb][:, :],
                                dq_b[:, i2 * 128:(i2 + 1) * 128], idn[:, :]))
            p.on("tensor", ttr)
            dtn = s_dt.n

            oslot = dq_idx["o"] % 4
            dq_idx["o"] += 1
            awo = dqo_slots.acquire(oslot)
            cbe = s_de.inc(1)

            def tev(sc, tb=tb, oslot=oslot, dtn=dtn, awo=awo, cbe=cbe):
                s_dt.wait(sc, dtn)
                awo(sc)
                cbe(sc.copy(dq_o[:, oslot, :], ps_t[tb][:, :]))
            p.on("scalar", tev)
            pst_slots.release_at(tb, s_de)
            den = s_de.n
            cbo = s_do.inc(16)

            def tout(g, i2=i2, oslot=oslot, den=den, cbo=cbo, out_hbm=out_hbm,
                     oloc=oloc, in0=in0, oprev=s_do.n - 16):
                s_de.wait(g, den)
                s_do.wait(g, oprev)
                cbo(g.dma_start(
                    out=out_hbm[in0 + i2 * 128:in0 + (i2 + 1) * 128,
                                oloc * 128:(oloc + 1) * 128],
                    in_=dq_o[:, oslot, :]))
            p.on("sync", tout)
            dqo_slots.release_at(oslot, s_do)
        dqb_slots.release_at(0, s_dt)

    uidx = 0
    for oc in range(11):
        dequant_unit(uidx, gup[oc], gusT[oc], 0, 256, 0, wguTs, oc, 0)
        uidx += 1
    for oc in range(2):
        for sub, (pk0, npk) in enumerate([(0, 256), (256, 256), (512, 192)]):
            dequant_unit(uidx, dnp[oc], dnsT[oc], pk0, npk, pk0 // 16,
                         wdnTs, oc, pk0 * 8)
            uidx += 1

    def fin(g, vo=s_od.n, vd=s_do.n):
        s_od.wait(g, vo)
        s_do.wait(g, vd)
    p.on("gpsimd", fin)

    return p.finish()



NQT = S // 512
NKBMAX = S // 128


def build_launch_attn(npairs=4, dbg=()):
    p = Prog("ATT")
    qT = p.dram("qT", [4, 128, S], BF16)
    kT = p.dram("kT", [4, 128, S], BF16)
    vblk = p.dram("vblk", [4, 128, NKBMAX * 128], BF16)
    masks = p.dram("masks", [128, 4 * 512], F32)
    masks2 = p.dram("masks2", [128, 4 * 512], F32)
    idb = p.dram("idb", [128, 128], BF16)
    outT = p.dram("outT", [4, 128, S], BF16, "ExternalOutput")
    dbgo = p.dram("dbgo", [128, 2 * S + 2 * NQT + 32], F32, "ExternalOutput")

    qs = p.sb("qs", [128, S], BF16)
    ks = p.sb("ks", [128, S], BF16)
    vs = p.sb("vs", [128, NKBMAX * 128], BF16)
    sq = p.sb("sq", [128, S], BF16)
    msk = p.sb("msk", [128, 4, 512], F32)
    r2q = p.sb("r2q", [1, S], F32)
    r2k = p.sb("r2k", [1, S], F32)
    kmax2 = p.sb("kmax2", [1, 1], F32)
    qmax2 = p.sb("qmax2", [1, NQT], F32)
    nbt = p.sb("nbt", [1, NQT], F32)
    nbt_b = p.sb("nbt_b", [128, NQT], F32)
    m8x = p.sb("m8x", [1, 17 * 8], F32)
    linv = p.sb("linv", [1, 512], F32)
    lb = p.sb("lb", [128, 512], F32)
    pT = p.sb("pT", [128, 2, 512], BF16)
    obuf = p.sb("obuf", [128, 2, 512], BF16)
    ones_b = p.sb("ones_b", [128, 1], BF16)
    ones_r = p.sb("ones_r", [1, 128], F32)
    msk2 = p.sb("msk2", [128, 4, 512], F32)
    idbs = p.sb("idbs", [128, 128], BF16)
    Mq = p.sb("Mq", [128, 64], F32)
    mq_bf = p.sb("mq_bf", [128, 64], BF16)
    tmax = p.sb("tmax", [128, 1], F32)
    negM = p.sb("negM", [1, S], BF16)
    negb = p.sb("negb", [128, 512], F32)
    ones_rb = p.sb("ones_rb", [1, 128], BF16)

    ps_S = [p.ps("ps_S0", [128, 512]), p.ps("ps_S1", [128, 512])]
    ps_O = p.ps("ps_O", [128, 512])
    ps_l = p.ps("ps_l", [1, 512])
    ps_lb = p.ps("ps_lb", [128, 512])
    ps_row = p.ps("ps_row", [1, 512])
    ps_ng = p.ps("ps_ng", [1, 512])
    ps_nb = p.ps("ps_nb", [128, 512])

    s_in = p.sem("s_in")
    s_sq = p.sem("s_sq")
    s_rmm = p.sem("s_rmm")
    s_r2 = p.sem("s_r2")
    s_bt = p.sem("s_bt")
    s_S = p.sem("s_S")
    s_mk = p.sem("s_mk")
    s_ex = p.sem("s_ex")
    s_av = p.sem("s_av")
    s_li = p.sem("s_li")
    s_lb = p.sem("s_lb")
    s_on = p.sem("s_on")
    s_out = p.sem("s_out")
    s_p1 = p.sem("s_p1")
    s_mx = p.sem("s_mx")
    s_tr = p.sem("s_tr")
    s_ng = p.sem("s_ng")
    s_nbm = p.sem("s_nbm")
    s_nb = p.sem("s_nb")

    p.on("gpsimd", lambda g: g.memset(ones_b[:, :], 1.0))
    p.on("gpsimd", lambda g: g.memset(ones_r[:, :], 1.0))
    p.on("gpsimd", lambda g: g.memset(ones_rb[:, :], 1.0))
    cbm = s_in.inc(16)
    p.on("gpsimd", lambda g, cbm=cbm: cbm(g.dma_start(
        out=msk[:, :, :], in_=masks.rearrange("p (r c) -> p r c", r=4))))
    cbm2 = s_in.inc(16)
    p.on("gpsimd", lambda g, cbm2=cbm2: cbm2(g.dma_start(
        out=msk2[:, :, :], in_=masks2.rearrange("p (r c) -> p r c", r=4))))
    cbmi = s_in.inc(16)
    p.on("gpsimd", lambda g, cbmi=cbmi: cbmi(g.dma_start(
        out=idbs[:, :], in_=idb[:, :])))

    pT_slots = Slots(2)
    ob_slots = Slots(2)
    bank_slots = Slots(2)
    psO_slot = Slots(1)
    psl_slot = Slots(1)
    pslb_slot = Slots(1)
    ex_cnt = {"n": 0}

    for pair in range(npairs):
        aw = psO_slot.acquire(0)
        cb1, cb2, cb3 = s_in.inc(16), s_in.inc(16), s_in.inc(16)

        def ld(g, pair=pair, cb1=cb1, cb2=cb2, cb3=cb3,
               prev_out=(pair * NQT * 16), iprev=s_in.n - 48):
            if pair > 0:
                s_out.wait(g, prev_out)
            s_in.wait(g, iprev)
            cb1(g.dma_start(out=qs[:, :], in_=qT[pair]))
            s_in.wait(g, iprev + 16)
            cb2(g.dma_start(out=ks[:, :], in_=kT[pair]))
            s_in.wait(g, iprev + 32)
            cb3(g.dma_start(out=vs[:, :], in_=vblk[pair]))
        p.on("gpsimd", ld)
        inn = s_in.n

        st_p1 = 0
        for qb in range(S // 128):
            nkt = qb // 4 + 1
            for kt in range(nkt):
                bank = st_p1 % 2
                st_p1 += 1
                awb = bank_slots.acquire(bank)
                cbP = s_p1.inc(1)

                def p1mm(t, qb=qb, kt=kt, bank=bank, awb=awb, cbP=cbP,
                         inn=inn, first=(qb == 0 and kt == 0)):
                    if first:
                        s_in.wait(t, inn)
                    awb(t)
                    cbP(t.matmul(ps_S[bank][:, :],
                                 qs[:, qb * 128:(qb + 1) * 128],
                                 ks[:, kt * 512:(kt + 1) * 512],
                                 start=True, stop=True,
                                 skip_group_check=True))
                p.on("tensor", p1mm)
                pn = s_p1.n
                cbX = s_mx.inc(1)

                def p1max(v, qb=qb, kt=kt, bank=bank, pn=pn, cbX=cbX,
                          diag=(kt == qb // 4)):
                    s_p1.wait(v, pn)
                    if diag:
                        v.tensor_tensor(ps_S[bank][:, :], ps_S[bank][:, :],
                                        msk2[:, qb % 4, :], OP.add)
                    if kt == 0:
                        cbX(v.tensor_reduce(Mq[:, qb:qb + 1],
                                            ps_S[bank][:, :],
                                            mybir.AxisListType.X, OP.max))
                    else:
                        v.tensor_reduce(tmax[:, :], ps_S[bank][:, :],
                                        mybir.AxisListType.X, OP.max)
                        cbX(v.tensor_max(Mq[:, qb:qb + 1], Mq[:, qb:qb + 1],
                                         tmax[:, :]))
                p.on("vector", p1max)
                bank_slots.release_at(bank, s_mx)
        cbC = s_mx.inc(1)

        def p1cast(v, need=s_mx.n - 1, cbC=cbC):
            s_mx.wait(v, need)
            cbC(v.tensor_copy(mq_bf[:, :], Mq[:, :]))
        p.on("vector", p1cast)
        castn = s_mx.n
        ngbase = s_ng.n
        trbase = s_tr.n
        for g in range(NQT):
            for j in range(4):
                qb = g * 4 + j
                cbT = s_tr.inc(1)

                def p1tr(t, qb=qb, j=j, g=g, cbT=cbT, castn=castn,
                         ngbase=ngbase):
                    if qb == 0:
                        s_mx.wait(t, castn)
                    if j == 0 and g > 0:
                        s_ng.wait(t, ngbase + g)
                    cbT(t.matmul(ps_ng[0:1, j * 128:(j + 1) * 128],
                                 mq_bf[:, qb:qb + 1], idbs[:, :],
                                 start=True, stop=True,
                                 skip_group_check=True))
                p.on("tensor", p1tr)
            cbNG = s_ng.inc(1)

            def p1ev(v, g=g, cbNG=cbNG, need=trbase + (g + 1) * 4):
                s_tr.wait(v, need)
                cbNG(v.tensor_scalar(negM[0:1, g * 512:(g + 1) * 512],
                                     ps_ng[:, :], -1.0, None, OP.mult))
            p.on("vector", p1ev)

        if "nostats" in dbg:
            cbz = s_bt.inc(1)
            def zstat(g, cbz=cbz, inn=inn):
                s_in.wait(g, inn)
                cbz(g.memset(nbt_b[:, :], 0.0))
            p.on("gpsimd", zstat)
            btn = s_bt.n
        cbs1, cbs2 = (s_sq.inc(1), s_sq.inc(1)) if "nostats" not in dbg else (None, None)

        def sq_op(sc, inn=inn, cbs1=cbs1, cbs2=cbs2, pair=pair,
                  rprev=(pair * 32)):
            s_in.wait(sc, inn)
            if pair > 0:
                s_r2.wait(sc, rprev)
            cbs1(sc.activation(sq[:, :], qs[:, :], AF.Square))
        if "nostats" not in dbg:
            p.on("scalar", sq_op)
        sqn1 = s_sq.n - 1

        for half in (range(2) if "nostats" not in dbg else []):
            src = sq
            dst = r2q if half == 0 else r2k
            for i in range(NQT):
                cbr = s_rmm.inc(1)

                def rmm(t, i=i, half=half, cbr=cbr, need=(s_sq.n - 1 + half),
                        evn=(s_r2.n)):
                    s_sq.wait(t, need)
                    s_r2.wait(t, evn)
                    cbr(t.matmul(ps_row[:, :], ones_b[:, :],
                                 sq[:, i * 512:(i + 1) * 512],
                                 start=True, stop=True))
                p.on("tensor", rmm)
                cbe = s_r2.inc(1)

                def rev(v, i=i, dst=dst, cbe=cbe, need=s_rmm.n):
                    s_rmm.wait(v, need)
                    cbe(v.tensor_copy(dst[:, i * 512:(i + 1) * 512],
                                      ps_row[:, :]))
                p.on("vector", rev)
            if half == 0:
                def sqk(sc, need=s_r2.n, cbs2=cbs2):
                    s_r2.wait(sc, need)
                    cbs2(sc.activation(sq[:, :], ks[:, :], AF.Square))
                p.on("scalar", sqk)

        cbb = s_bt.inc(1) if "nostats" not in dbg else None

        def btrow(v, need=s_r2.n, cbb=cbb, pair=pair):
            s_r2.wait(v, need)
            s_bt.wait(v, 5 * pair)
            for i in range(NQT):
                v.max(m8x[:, i * 8:(i + 1) * 8], r2q[:, i * 512:(i + 1) * 512])
            v.max(m8x[:, 128:136], r2k[:, :])
            v.drain()
            cbb(v.tensor_scalar(nbt[:, :], m8x[0:1, 0:128:8],
                                m8x[0:1, 128:129], 1.0 / DH, OP.mult, OP.mult))
        if "nostats" not in dbg:
            p.on("vector", btrow)

        cbb2 = s_bt.inc(1) if "nostats" not in dbg else None

        def btsqrt(sc, need=s_bt.n - 1, cbb2=cbb2):
            s_bt.wait(sc, need)
            cbb2(sc.activation(nbt[:, :], nbt[:, :], AF.Sqrt))
        if "nostats" not in dbg:
            p.on("scalar", btsqrt)
        cbb3 = s_bt.inc(1) if "nostats" not in dbg else None

        def btneg(v, need=s_bt.n - 1, cbb3=cbb3):
            s_bt.wait(v, need)
            cbb3(v.tensor_scalar(nbt[:, :], nbt[:, :], -1.0, None, OP.mult))
        if "nostats" not in dbg:
            p.on("vector", btneg)
        cbb4 = s_bt.inc(1) if "nostats" not in dbg else None
        awlb2 = pslb_slot.acquire(0)

        def btbc(t, need=s_bt.n - 1, cbb4=cbb4, awlb2=awlb2):
            s_bt.wait(t, need)
            awlb2(t)
            cbb4(t.matmul(ps_lb[:, 0:NQT], ones_r[:, :], nbt[:, :],
                          start=True, stop=True, skip_group_check=True))
        if "nostats" not in dbg:
            p.on("tensor", btbc)
        cbb5 = s_bt.inc(1) if "nostats" not in dbg else None

        def btev(v, need=s_bt.n - 1, cbb5=cbb5):
            s_bt.wait(v, need)
            cbb5(v.tensor_copy(nbt_b[:, :], ps_lb[:, 0:NQT]))
        if "nostats" not in dbg:
            p.on("vector", btev)
            pslb_slot.release_at(0, s_bt)
            btn = s_bt.n
        if "dump" in dbg and pair == 0:
            cbds = [s_out.inc(16) for _ in range(6)]
            def ddump(g, cbds=cbds, btn=btn, oprev=s_out.n - 96):
                s_bt.wait(g, btn)
                s_out.wait(g, oprev)
                cbds[0](g.dma_start(out=dbgo[0:1, 0:S], in_=r2q[:, :]))
                cbds[1](g.dma_start(out=dbgo[0:1, S:2 * S], in_=r2k[:, :]))
                cbds[2](g.dma_start(out=dbgo[0:1, 2 * S:2 * S + NQT], in_=nbt[:, :]))
                cbds[3](g.dma_start(out=dbgo[:, 2 * S + NQT:2 * S + 2 * NQT],
                                    in_=nbt_b[:, :]))
                cbds[4](g.dma_start(out=dbgo[0:1, 2 * S + 2 * NQT:2 * S + 2 * NQT + 16],
                                    in_=qmax2[:, :]))
                cbds[5](g.dma_start(out=dbgo[0:1, 2 * S + 2 * NQT + 16:2 * S + 2 * NQT + 17],
                                    in_=kmax2[:, :]))
            p.on("gpsimd", ddump)

        for qt in range(NQT):
            nkb = 4 * qt + 4
            avn0 = s_av.n
            nbprev = s_nb.n
            cbBM = s_nbm.inc(1)

            def nbmm(t, cbBM=cbBM, nbprev=nbprev, ngn=(ngbase + qt + 1),
                     qt=qt):
                s_ng.wait(t, ngn)
                if nbprev > 0:
                    s_nb.wait(t, nbprev)
                cbBM(t.matmul(ps_nb[:, :], ones_rb[:, :],
                              negM[0:1, qt * 512:(qt + 1) * 512],
                              start=True, stop=True, skip_group_check=True))
            p.on("tensor", nbmm)
            nbmn = s_nbm.n
            mkprev = s_mk.n
            cbNB = s_nb.inc(1)

            def nbev(v, cbNB=cbNB, nbmn=nbmn, mkprev=mkprev):
                s_nbm.wait(v, nbmn)
                if mkprev > 0:
                    s_mk.wait(v, mkprev)
                cbNB(v.tensor_copy(negb[:, :], ps_nb[:, :]))
            p.on("vector", nbev)
            nbn = s_nb.n
            for kb in range(nkb):
                bank = kb % 2
                diag = kb >= 4 * qt
                awb = bank_slots.acquire(bank)
                cbS = s_S.inc(1)

                def mmS(t, kb=kb, qt=qt, bank=bank, awb=awb, cbS=cbS, btn=btn,
                        inn=inn):
                    if kb == 0:
                        s_in.wait(t, inn)
                        s_bt.wait(t, btn)
                    awb(t)
                    cbS(t.matmul(ps_S[bank][:, :], ks[:, kb * 128:(kb + 1) * 128],
                                 qs[:, qt * 512:(qt + 1) * 512],
                                 start=True, stop=True, skip_group_check=True))
                p.on("tensor", mmS)
                Sn = s_S.n

                cbM = s_mk.inc(1)

                def mmsk(v, bank=bank, diag=diag, r=kb - 4 * qt, Sn=Sn,
                         cbM=cbM, nbn=nbn):
                    s_S.wait(v, Sn)
                    s_nb.wait(v, nbn)
                    if diag:
                        v.tensor_tensor(ps_S[bank][:, :], ps_S[bank][:, :],
                                        msk[:, r, :], OP.add)
                    cbM(v.tensor_tensor(ps_S[bank][:, :], ps_S[bank][:, :],
                                        negb[:, :], OP.add))
                p.on("vector", mmsk)
                mkn = s_mk.n

                pslot = kb % 2
                awp = pT_slots.acquire(pslot)
                cbE = s_ex.inc(1)
                ex_cnt["n"] += 1

                def expo(sc, bank=bank, pslot=pslot, qt=qt, Sn=Sn, mkn=mkn,
                         awp=awp, cbE=cbE):
                    if mkn is None:
                        s_S.wait(sc, Sn)
                    else:
                        s_mk.wait(sc, mkn)
                    awp(sc)
                    cbE(sc.activation(pT[:, pslot, :], ps_S[bank][:, :],
                                      AF.Exp, bias=nbt_b[:, qt:qt + 1],
                                      scale=RSQD))
                p.on("scalar", expo)
                bank_slots.release_at(bank, s_ex)
                exn = s_ex.n

                awO = psO_slot.acquire(0) if kb == 0 else None
                awl = psl_slot.acquire(0) if kb == 0 else None
                cbA = s_av.inc(1)

                def mmAV(t, kb=kb, pslot=pslot, exn=exn, nkb=nkb, cbA=cbA,
                         awO=awO, awl=awl):
                    s_ex.wait(t, exn)
                    if awO is not None:
                        awO(t)
                        awl(t)
                    t.matmul(ps_O[:, :], vs[:, kb * 128:(kb + 1) * 128],
                             pT[:, pslot, :], start=(kb == 0),
                             stop=(kb == nkb - 1), skip_group_check=True)
                    cbA(t.matmul(ps_l[:, :], ones_b[:, :], pT[:, pslot, :],
                                 start=(kb == 0), stop=(kb == nkb - 1),
                                 skip_group_check=True))
                p.on("tensor", mmAV)
                pT_slots.release_at(pslot, s_av)

            avn = s_av.n
            cbL = s_li.inc(1)

            def recip(v, avn=avn, cbL=cbL):
                s_av.wait(v, avn)
                v.reciprocal(linv[:, :], ps_l[:, :])
                v.drain()
                cbL(v.tensor_scalar(linv[:, :], linv[:, :], 1.0, None, OP.mult))
            p.on("vector", recip)
            psl_slot.release_at(0, s_li)
            lin = s_li.n
            awlb = pslb_slot.acquire(0)
            cbLB = s_lb.inc(1)

            def lbc(t, lin=lin, awlb=awlb, cbLB=cbLB):
                s_li.wait(t, lin)
                awlb(t)
                cbLB(t.matmul(ps_lb[:, :], ones_r[:, :], linv[:, :],
                              start=True, stop=True, skip_group_check=True))
            p.on("tensor", lbc)
            lbn = s_lb.n
            cbLE = s_lb.inc(1)

            def lbev(v, lbn=lbn, cbLE=cbLE):
                s_lb.wait(v, lbn)
                cbLE(v.tensor_copy(lb[:, :], ps_lb[:, :]))
            p.on("vector", lbev)
            pslb_slot.release_at(0, s_lb)
            lbn2 = s_lb.n

            oslot = qt % 2
            awo = ob_slots.acquire(oslot)
            cbON = s_on.inc(1)

            def onorm(v, oslot=oslot, lbn2=lbn2, awo=awo, cbON=cbON):
                s_lb.wait(v, lbn2)
                awo(v)
                cbON(v.tensor_tensor(obuf[:, oslot, :], ps_O[:, :], lb[:, :],
                                     OP.mult))
            p.on("vector", onorm)
            psO_slot.release_at(0, s_on)
            onn = s_on.n
            cbOD = s_out.inc(16)

            def odma(g, pair=pair, qt=qt, oslot=oslot, onn=onn, cbOD=cbOD,
                      oprev=s_out.n - 16):
                s_on.wait(g, onn)
                s_out.wait(g, oprev)
                cbOD(g.dma_start(
                    out=outT[pair][:, qt * 512:(qt + 1) * 512],
                    in_=obuf[:, oslot, :]))
            p.on("sync", odma)
            ob_slots.release_at(oslot, s_out)

    def fin(g, v=s_out.n):
        s_out.wait(g, v)
    p.on("gpsimd", fin)
    return p.finish()



def build_launch_C():
    p = Prog("C")
    xT = p.dram("xT", [D, TPC], F32)
    a0t = p.dram("a0t", [D, TPC], BF16)
    wo = p.dram("wo", [16, 128, NCH * 128], F32)
    wq1 = p.dram("wq1", [16, 128, NCH * 128], F32)
    wk1 = p.dram("wk1", [16, 128, NCH * 128], F32)
    wv1 = p.dram("wv1", [4, 128, NCH * 512], F32)
    bq1 = p.dram("bq1", [128, 16], F32)
    bk1 = p.dram("bk1", [128, 16], F32)
    bv1 = p.dram("bv1", [1, D], F32)
    wln1 = p.dram("wln1", [128, 16], F32)
    ropeC = p.dram("ropeC", [128, TPC], BF16)
    ropeS = p.dram("ropeS", [128, TPC], BF16)
    rswap = p.dram("rswap", [128, 128], BF16)

    x1T = p.dram("x1T", [D, TPC], F32, "ExternalOutput")
    xh1T = p.dram("xh1T", [D, TPC], F32, "ExternalOutput")
    q1t = p.dram("q1t", [D, TPC], BF16, "ExternalOutput")
    k1t = p.dram("k1t", [D, TPC], BF16, "ExternalOutput")
    v1 = p.dram("v1", [TPC, D], BF16, "ExternalOutput")

    TT = 512
    act = p.sb("act", [128, NCH, TPC], BF16)
    xc = p.sb("xc", [128, 4, TT], F32)
    xb = p.sb("xb", [128, 2, TT], BF16)
    xq = p.sb("xq", [128, 2, TT], BF16)
    mrow = p.sb("mrow", [1, TT], F32)
    vrow = p.sb("vrow", [1, TT], F32)
    srow = p.sb("srow", [1, TT], F32)
    arow = p.sb("arow", [1, TT], F32)
    brow = p.sb("brow", [1, TT], F32)
    tmpf = p.sb("tmpf", [128, TT], F32)
    tmp2 = p.sb("tmp2", [128, 2, TT], F32)
    wlns = p.sb("wlns", [128, 16], F32)
    bqs = p.sb("bqs", [128, 16], F32)
    bks = p.sb("bks", [128, 16], F32)
    bvs = p.sb("bvs", [1, D], BF16)
    rC = p.sb("rC", [128, TPC], BF16)
    rS = p.sb("rS", [128, TPC], BF16)
    rsw = p.sb("rsw", [128, 128], BF16)
    ones_b = p.sb("ones_b", [128, 1], BF16)
    ones_r = p.sb("ones_r", [1, 128], F32)
    ones_rb = p.sb("ones_rb", [1, 128], BF16)
    wt = p.sb("wt", [128, 2, NCH * 128], BF16)
    wmv = p.sb("wmv", [128, 1, NCH * 512], BF16)
    obuf = p.sb("obuf", [128, 4, 512], BF16)
    obf32 = p.sb("obf32", [128, 2, 512], F32)

    ps_a = p.ps("ps_a", [128, 512])
    ps_b = p.ps("ps_b", [128, 512])
    ps_s1 = p.ps("ps_s1", [1, 512])
    ps_s2 = p.ps("ps_s2", [1, 512])
    ps_r1 = p.ps("ps_r1", [128, 512])
    ps_r2 = p.ps("ps_r2", [128, 512])
    ps_sw = p.ps("ps_sw", [128, 2, 512])
    ps_ab = [ps_a, ps_b]

    s_pre = p.sem("s_pre")
    s_w = p.sem("s_w")
    s_pm = p.sem("s_pm")
    s_ev = p.sem("s_ev")
    s_od = p.sem("s_od")
    s_xc = p.sem("s_xc")
    s_cast = p.sem("s_cast")
    s_stmm = p.sem("s_stmm")
    s_vr = p.sem("s_vr")
    s_sq = p.sem("s_sq")
    s_row = p.sem("s_row")
    s_bc = p.sem("s_bc")
    s_ap = p.sem("s_ap")
    s_sw = p.sem("s_sw")
    s_rp = p.sem("s_rp")

    _pre_cbs = [s_pre.inc(16) for _ in range(7 + NCH)]

    def pre(g, cbs=_pre_cbs):
        g.memset(ones_b[:, :], 1.0)
        g.memset(ones_r[:, :], 1.0)
        g.memset(ones_rb[:, :], 1.0)
        k = 0
        for (dst, src) in [(wlns, wln1), (bqs, bq1), (bks, bk1), (rC, ropeC),
                           (rS, ropeS), (rsw, rswap)]:
            g.wait_ge(s_pre.h, 16 * k)
            cbs[k](g.dma_start(out=dst[:, :], in_=src[:, :]))
            k += 1
        g.wait_ge(s_pre.h, 16 * k)
        cbs[k](g.dma_start(out=bvs[:, :], in_=bv1[:, :]))
        k += 1
        for f in range(NCH):
            g.wait_ge(s_pre.h, 16 * k)
            cbs[k](g.dma_start(out=act[:, f, :],
                               in_=a0t[f * 128:(f + 1) * 128, :]))
            k += 1
    p.on("gpsimd", pre)
    pre_n = s_pre.n

    wt_slots = Slots(2)
    wmv_slots = Slots(1)
    bank_slots = Slots(2)
    ob_slots = Slots(4)
    ob32_slots = Slots(2)
    xc_slots = Slots(4)
    t2_slots = Slots(2)
    st = {"pm": 0, "ev": 0, "w": 0, "ev32": 0, "xc": 0, "t2": 0}

    def xload(src_hbm, f, tt):
        xslot = st["xc"] % 4
        st["xc"] += 1
        awx = xc_slots.acquire(xslot)
        cbx = s_xc.inc(16)

        def op(g, f=f, tt=tt, xslot=xslot, awx=awx, cbx=cbx, src_hbm=src_hbm,
               xprev=s_xc.n - 16):
            awx(g)
            s_xc.wait(g, xprev)
            cbx(g.dma_start(
                out=xc[:, xslot, :],
                in_=src_hbm[f * 128:(f + 1) * 128, tt * TT:(tt + 1) * TT]))
        p.on("gpsimd", op)
        return xslot, s_xc.n

    for ob in range(16):
        wslot = st["w"] % 2
        st["w"] += 1
        aw = wt_slots.acquire(wslot)
        cbw = s_w.inc(16)
        wneed = s_w.n

        def wload(g, ob=ob, wslot=wslot, aw=aw, cbw=cbw, wprev=wneed - 16):
            aw(g)
            s_w.wait(g, wprev)
            cbw(g.dma_start(out=wt[:, wslot, :], in_=wo[ob]))
        p.on("gpsimd", wload)
        for tt in range(4):
            xslot, xcn = xload(xT, ob, tt)
            bank = st["pm"] % 2
            st["pm"] += 1
            awb = bank_slots.acquire(bank)
            cbm = s_pm.inc(1)

            def mmop(t, tt=tt, bank=bank, wslot=wslot, wneed=wneed, awb=awb,
                     cbm=cbm, first=(ob == 0 and tt == 0), pre_n=pre_n):
                if first:
                    s_pre.wait(t, pre_n)
                s_w.wait(t, wneed)
                awb(t)
                for cc2 in range(NCH):
                    mm = t.matmul(ps_ab[bank][:, :],
                                  wt[:, wslot, cc2 * 128:(cc2 + 1) * 128],
                                  act[:, cc2, tt * 512:(tt + 1) * 512],
                                  start=(cc2 == 0), stop=(cc2 == NCH - 1))
                cbm(mm)
            p.on("tensor", mmop)
            pmneed = s_pm.n

            oslot = st["ev32"] % 2
            st["ev32"] += 1
            awo = ob32_slots.acquire(oslot)
            cbe = s_ev.inc(1)

            def evop(v, bank=bank, oslot=oslot, xslot=xslot, pmneed=pmneed,
                     xcn=xcn, awo=awo, cbe=cbe):
                s_pm.wait(v, pmneed)
                s_xc.wait(v, xcn)
                awo(v)
                cbe(v.tensor_tensor(obf32[:, oslot, :], ps_ab[bank][:, :],
                                    xc[:, xslot, :], OP.add))
            p.on("vector", evop)
            bank_slots.release_at(bank, s_ev)
            xc_slots.release_at(xslot, s_ev)
            evneed = s_ev.n
            cbo = s_od.inc(16)

            def odop(g, ob=ob, tt=tt, oslot=oslot, evneed=evneed, cbo=cbo,
                     oprev=s_od.n - 16):
                s_ev.wait(g, evneed)
                s_od.wait(g, oprev)
                cbo(g.dma_start(
                    out=x1T[ob * 128:(ob + 1) * 128, tt * TT:(tt + 1) * TT],
                    in_=obf32[:, oslot, :]))
            p.on("sync", odop)
            ob32_slots.release_at(oslot, s_od)
        wt_slots.release_at(wslot, s_pm)
    c1_pm = s_pm.n
    c1_od = s_od.n

    p.on("gpsimd", lambda g, v=c1_od: s_od.wait(g, v))
    for tt in range(4):
        for f in range(NCH):
            i = tt * NCH + f
            xslot, xcn = xload(x1T, f, tt)
            cb = s_cast.inc(1)

            def cast_op(sc, f=f, i=i, xslot=xslot, xcn=xcn, cb=cb):
                s_xc.wait(sc, xcn)
                if i >= 2:
                    s_stmm.wait(sc, i - 1)
                sc.copy(xb[:, f % 2, :], xc[:, xslot, :])
                cb(sc.activation(xq[:, f % 2, :], xc[:, xslot, :], AF.Square))
            p.on("scalar", cast_op)
            xc_slots.release_at(xslot, s_cast)
            cb2 = s_stmm.inc(1)

            def st_mm(t, f=f, i=i, tt=tt, cb2=cb2):
                s_cast.wait(t, i + 1)
                if f == 0 and tt > 0:
                    s_row.wait(t, tt)
                t.matmul(ps_s1[:, :], ones_b[:, :], xb[:, f % 2, :],
                         start=(f == 0), stop=(f == NCH - 1))
                cb2(t.matmul(ps_s2[:, :], ones_b[:, :], xq[:, f % 2, :],
                             start=(f == 0), stop=(f == NCH - 1)))
            p.on("tensor", st_mm)

        cbv = s_vr.inc(1)

        def rows_a(v, tt=tt, cbv=cbv):
            s_stmm.wait(v, (tt + 1) * NCH)
            s_sq.wait(v, tt)
            v.tensor_scalar(mrow[:, :], ps_s1[:, :], 1.0 / D, None, OP.mult)
            v.tensor_tensor(srow[:, :], mrow[:, :], mrow[:, :], OP.mult)
            v.tensor_scalar(vrow[:, :], ps_s2[:, :], 1.0 / D, LN_EPS,
                            OP.mult, OP.add)
            cbv(v.tensor_tensor(vrow[:, :], vrow[:, :], srow[:, :],
                                OP.subtract))
        p.on("vector", rows_a)
        cbs = s_sq.inc(1)

        def rows_b(sc, tt=tt, cbs=cbs):
            s_vr.wait(sc, tt + 1)
            s_bc.wait(sc, tt)
            cbs(sc.activation(arow[:, :], vrow[:, :], AF.Sqrt))
        p.on("scalar", rows_b)
        cbr = s_row.inc(1)

        def rows_c(v, tt=tt, cbr=cbr):
            s_sq.wait(v, tt + 1)
            v.reciprocal(arow[:, :], arow[:, :])
            v.drain()
            cbr(v.scalar_tensor_tensor(brow[:, :], mrow[:, :], -1.0,
                                       arow[:, :], OP.mult, OP.mult))
        p.on("vector", rows_c)
        cbb = s_bc.inc(1)

        def bc_op(t, tt=tt, cbb=cbb):
            s_row.wait(t, tt + 1)
            if tt > 0:
                s_ap.wait(t, tt * NCH)
            t.matmul(ps_r1[:, :], ones_r[:, :], arow[:, :], start=True,
                     stop=True, skip_group_check=True)
            cbb(t.matmul(ps_r2[:, :], ones_r[:, :], brow[:, :], start=True,
                         stop=True, skip_group_check=True))
        p.on("tensor", bc_op)

        for f in range(NCH):
            xslot, xcn = xload(x1T, f, tt)
            t2s = st["t2"] % 2
            st["t2"] += 1
            awt2 = t2_slots.acquire(t2s)
            cba = s_ap.inc(1)

            def apply_op(v, f=f, tt=tt, xslot=xslot, xcn=xcn, t2s=t2s,
                         awt2=awt2, cba=cba):
                s_bc.wait(v, tt + 1)
                s_xc.wait(v, xcn)
                awt2(v)
                v.tensor_tensor(tmpf[:, :], xc[:, xslot, :], ps_r1[:, :],
                                OP.mult)
                cba(v.tensor_tensor(tmp2[:, t2s, :], tmpf[:, :], ps_r2[:, :],
                                    OP.add))
            p.on("vector", apply_op)
            xc_slots.release_at(xslot, s_ap)
            apn = s_ap.n
            cbo = s_od.inc(16)

            def xhout(g, f=f, tt=tt, t2s=t2s, cbo=cbo, apn=apn,
                      oprev=s_od.n - 16):
                s_ap.wait(g, apn)
                s_od.wait(g, oprev)
                cbo(g.dma_start(
                    out=xh1T[f * 128:(f + 1) * 128, tt * TT:(tt + 1) * TT],
                    in_=tmp2[:, t2s, :]))
            p.on("sync", xhout)
            t2_slots.release_at(t2s, s_od)
    ln_ap = s_ap.n
    ln_od = s_od.n

    p.on("gpsimd", lambda g, v=ln_od: s_od.wait(g, v))
    for tt in range(4):
        for f in range(NCH):
            i = tt * NCH + f
            xslot, xcn = xload(xh1T, f, tt)
            cb = s_cast.inc(1)

            def sq_op(sc, i=i, xslot=xslot, xcn=xcn, cb=cb):
                s_xc.wait(sc, xcn)
                if i >= 2:
                    s_stmm.wait(sc, 64 + i - 1)
                cb(sc.activation(xq[:, i % 2, :], xc[:, xslot, :], AF.Square))
            p.on("scalar", sq_op)
            xc_slots.release_at(xslot, s_cast)
            cb2 = s_stmm.inc(1)

            def st_mm2(t, f=f, i=i, tt=tt, cb2=cb2):
                s_cast.wait(t, 64 + i + 1)
                if f == 0 and tt > 0:
                    s_row.wait(t, 4 + tt)
                cb2(t.matmul(ps_s2[:, :], ones_b[:, :], xq[:, i % 2, :],
                             start=(f == 0), stop=(f == NCH - 1)))
            p.on("tensor", st_mm2)
        cbv2 = s_vr.inc(1)

        def rows_v2(v, tt=tt, cbv2=cbv2, need=(64 + (tt + 1) * NCH)):
            s_stmm.wait(v, need)
            s_sq.wait(v, 4 + tt)
            cbv2(v.tensor_scalar(vrow[:, :], ps_s2[:, :], 1.0 / D, RMS_EPS,
                                 OP.mult, OP.add))
        p.on("vector", rows_v2)
        cbs = s_sq.inc(1)

        def rows_b2(sc, tt=tt, cbs=cbs, need=(4 + tt + 1)):
            s_vr.wait(sc, need)
            s_bc.wait(sc, 4 + tt)
            cbs(sc.activation(arow[:, :], vrow[:, :], AF.Sqrt))
        p.on("scalar", rows_b2)
        cbr = s_row.inc(1)

        def rows_c2(v, tt=tt, cbr=cbr):
            s_sq.wait(v, 4 + tt + 1)
            v.reciprocal(arow[:, :], arow[:, :])
            v.drain()
            cbr(v.tensor_scalar(arow[:, :], arow[:, :], 1.0, None, OP.mult))
        p.on("vector", rows_c2)
        cbb = s_bc.inc(1)

        def bc_op2(t, tt=tt, cbb=cbb, c1=c1_pm):
            s_row.wait(t, 4 + tt + 1)
            if tt == 0:
                s_pm.wait(t, c1)
                s_ap.wait(t, ln_ap)
            else:
                s_ap.wait(t, ln_ap + tt * NCH)
            cbb(t.matmul(ps_r1[:, :], ones_r[:, :], arow[:, :], start=True,
                         stop=True, skip_group_check=True))
        p.on("tensor", bc_op2)

        for f in range(NCH):
            xslot, xcn = xload(xh1T, f, tt)
            cba = s_ap.inc(1)

            def apply2(v, f=f, tt=tt, xslot=xslot, xcn=xcn, cba=cba):
                s_bc.wait(v, 4 + tt + 1)
                s_xc.wait(v, xcn)
                cba(v.scalar_tensor_tensor(
                    act[:, f, tt * TT:(tt + 1) * TT], xc[:, xslot, :],
                    wlns[:, f:f + 1], ps_r1[:, :], OP.mult, OP.mult))
            p.on("vector", apply2)
            xc_slots.release_at(xslot, s_ap)
    rms_ap = s_ap.n

    def qk_proj(w_hbm, bias_sb, out_hbm, do_rope):
        for ob in range(16):
            wslot = st["w"] % 2
            st["w"] += 1
            aw = wt_slots.acquire(wslot)
            cbw = s_w.inc(16)
            wneed = s_w.n

            def wload(g, ob=ob, wslot=wslot, aw=aw, cbw=cbw, w_hbm=w_hbm,
                      wprev=wneed - 16):
                aw(g)
                s_w.wait(g, wprev)
                cbw(g.dma_start(out=wt[:, wslot, :], in_=w_hbm[ob]))
            p.on("gpsimd", wload)
            for tt in range(4):
                bank = st["pm"] % 2
                st["pm"] += 1
                awb = bank_slots.acquire(bank)
                cbm = s_pm.inc(1)

                def mmop(t, tt=tt, bank=bank, wslot=wslot, wneed=wneed,
                         awb=awb, cbm=cbm, first=(ob == 0 and tt == 0),
                         rmsn=rms_ap):
                    if first:
                        s_ap.wait(t, rmsn)
                    s_w.wait(t, wneed)
                    awb(t)
                    for cc2 in range(NCH):
                        mm = t.matmul(ps_ab[bank][:, :],
                                      wt[:, wslot, cc2 * 128:(cc2 + 1) * 128],
                                      act[:, cc2, tt * 512:(tt + 1) * 512],
                                      start=(cc2 == 0), stop=(cc2 == NCH - 1))
                    cbm(mm)
                p.on("tensor", mmop)
                pmneed = s_pm.n

                oslot = st["ev"] % 4
                st["ev"] += 1
                awo = ob_slots.acquire(oslot)
                cbe = s_ev.inc(1)

                def evop(v, ob=ob, bank=bank, oslot=oslot, pmneed=pmneed,
                         awo=awo, cbe=cbe, bias_sb=bias_sb):
                    s_pm.wait(v, pmneed)
                    awo(v)
                    cbe(v.tensor_scalar(obuf[:, oslot, :], ps_ab[bank][:, :],
                                        bias_sb[:, ob:ob + 1], None, OP.add))
                p.on("vector", evop)
                evneed = s_ev.n

                if do_rope:
                    swb = st["pm"] % 2
                    st["pm"] += 1
                    awsw = bank_slots.acquire(swb)
                    cbsw = s_sw.inc(1)

                    def swmm(t, oslot=oslot, swb=swb, evneed=evneed,
                             awsw=awsw, cbsw=cbsw):
                        s_ev.wait(t, evneed)
                        awsw(t)
                        cbsw(t.matmul(ps_sw[:, swb, :], rsw[:, :],
                                      obuf[:, oslot, :], start=True,
                                      stop=True, skip_group_check=True))
                    p.on("tensor", swmm)
                    swn = s_sw.n
                    t2s = st["t2"] % 2
                    st["t2"] += 1
                    awt2 = t2_slots.acquire(t2s)
                    cbrp = s_rp.inc(1)

                    def rope_op(v, tt=tt, oslot=oslot, swb=swb, swn=swn,
                                t2s=t2s, awt2=awt2, cbrp=cbrp):
                        s_sw.wait(v, swn)
                        awt2(v)
                        v.tensor_tensor(tmpf[:, :], obuf[:, oslot, :],
                                        rC[:, tt * 512:(tt + 1) * 512],
                                        OP.mult)
                        v.tensor_tensor(tmp2[:, t2s, :], ps_sw[:, swb, :],
                                        rS[:, tt * 512:(tt + 1) * 512],
                                        OP.mult)
                        cbrp(v.tensor_tensor(obuf[:, oslot, :], tmpf[:, :],
                                             tmp2[:, t2s, :], OP.add))
                    p.on("vector", rope_op)
                    bank_slots.release_at(swb, s_rp)
                    bank_slots.release_at(bank, s_rp)
                    t2_slots.release_at(t2s, s_rp)
                    fin_sem, fin_n = s_rp, s_rp.n
                else:
                    bank_slots.release_at(bank, s_ev)
                    fin_sem, fin_n = s_ev, s_ev.n

                cbo = s_od.inc(16)

                def odop(g, ob=ob, tt=tt, oslot=oslot, fin_sem=fin_sem,
                         fin_n=fin_n, cbo=cbo, out_hbm=out_hbm,
                         oprev=s_od.n - 16):
                    fin_sem.wait(g, fin_n)
                    s_od.wait(g, oprev)
                    cbo(g.dma_start(
                        out=out_hbm[ob * 128:(ob + 1) * 128,
                                    tt * TT:(tt + 1) * TT],
                        in_=obuf[:, oslot, :]))
                p.on("sync", odop)
                ob_slots.release_at(oslot, s_od)
            wt_slots.release_at(wslot, s_pm)

    qk_proj(wq1, bqs, q1t, True)
    qk_proj(wk1, bks, k1t, True)

    for oc in range(4):
        mslot = 0
        aw = wmv_slots.acquire(mslot)
        cbw = s_w.inc(16)
        wneed = s_w.n

        def wvload(g, oc=oc, mslot=mslot, aw=aw, cbw=cbw, wprev=wneed - 16):
            aw(g)
            s_w.wait(g, wprev)
            cbw(g.dma_start(out=wmv[:, mslot, :], in_=wv1[oc]))
        p.on("gpsimd", wvload)
        for tk in range(16):
            bank = st["pm"] % 2
            st["pm"] += 1
            awb = bank_slots.acquire(bank)
            cbm = s_pm.inc(1)

            def mmop(t, oc=oc, tk=tk, bank=bank, mslot=mslot, wneed=wneed,
                     awb=awb, cbm=cbm):
                s_w.wait(t, wneed)
                awb(t)
                for cc2 in range(NCH):
                    t.matmul(ps_ab[bank][:, :],
                             act[:, cc2, tk * 128:(tk + 1) * 128],
                             wmv[:, mslot, cc2 * 512:(cc2 + 1) * 512],
                             start=(cc2 == 0), stop=False,
                             skip_group_check=True)
                cbm(t.matmul(ps_ab[bank][:, :], ones_rb[:, :],
                             bvs[:, oc * 512:(oc + 1) * 512], start=False,
                             stop=True, skip_group_check=True))
            p.on("tensor", mmop)
            pmneed = s_pm.n
            oslot = st["ev"] % 4
            st["ev"] += 1
            awo = ob_slots.acquire(oslot)
            cbe = s_ev.inc(1)

            def evop(v, bank=bank, oslot=oslot, pmneed=pmneed, awo=awo,
                     cbe=cbe):
                s_pm.wait(v, pmneed)
                awo(v)
                cbe(v.tensor_copy(obuf[:, oslot, :], ps_ab[bank][:, :]))
            p.on("vector", evop)
            bank_slots.release_at(bank, s_ev)
            evneed = s_ev.n
            cbo = s_od.inc(16)

            def odop(g, oc=oc, tk=tk, oslot=oslot, evneed=evneed, cbo=cbo,
                     oprev=s_od.n - 16):
                s_ev.wait(g, evneed)
                s_od.wait(g, oprev)
                cbo(g.dma_start(
                    out=v1[tk * 128:(tk + 1) * 128, oc * 512:(oc + 1) * 512],
                    in_=obuf[:, oslot, :]))
            p.on("sync", odop)
            ob_slots.release_at(oslot, s_od)
        wmv_slots.release_at(mslot, s_pm)

    def fin(g, v=s_od.n):
        s_od.wait(g, v)
    p.on("gpsimd", fin)
    return p.finish()



def build_launch_E(dbg=()):
    p = Prog("E")
    ctxT = p.dram("ctxT", [D, TPC], BF16)
    x1T = p.dram("x1T", [D, TPC], F32)
    xh1T = p.dram("xh1T", [D, TPC], F32)
    wo2 = p.dram("wo2", [16, 128, NCH * 128], F32)
    wln2 = p.dram("wln2", [128, 16], F32)
    wrt9 = p.dram("wrt9", [16, 128, 16], F32)
    wg = p.dram("wg", [8, 11, 128, NCH * 128], F32)
    wu = p.dram("wu", [8, 11, 128, NCH * 128], F32)
    wd = p.dram("wd", [8, 16, 128, 11 * 128], F32)
    wsg = p.dram("wsg", [44, 128, NCH * 128], F32)
    wsu = p.dram("wsu", [44, 128, NCH * 128], F32)
    wsd = p.dram("wsd", [16, 128, 44 * 128], F32)
    wpq = p.dram("wpq", [32, 128, NCH * 128], F32)
    wpv = p.dram("wpv", [8, 128, NCH * 256], F32)
    idf = p.dram("idf", [128, 128], F32)
    idb = p.dram("idb", [128, 128], BF16)

    _ik = "ExternalOutput" if "expose" in dbg else "Internal"
    xmT = p.dram("xmT", [D, TPC], F32, _ik)
    crowsH = p.dram("crowsH", [9, TPC], BF16, _ik)
    h2T = p.dram("h2T", [D, TPC], BF16, _ik)
    x3T = p.dram("x3T", [D, TPC], F32, "ExternalOutput")
    q2t = p.dram("q2t", [D, TPC], BF16, "ExternalOutput")
    k2t = p.dram("k2t", [D, TPC], BF16, "ExternalOutput")
    v2 = p.dram("v2", [TPC, D], BF16, "ExternalOutput")

    TT = 512
    big = p.sb("big", [128, 55296], BF16)
    act = big[:, 0:NCH * 2048].rearrange("p (c t) -> p c t", c=NCH)
    routed = big[:, 22528:55296].bitcast(F32).rearrange(
        "p (c t) -> p c t", c=NCH, t=1024)
    h2h = p.sb("h2h", [128, NCH, 1024], BF16)
    cbs_sb = p.sb("cbs_sb", [128, 1024], BF16)
    crows = None
    gs = p.sb("gs", [128, 2, 512], BF16)
    us = p.sb("us", [128, 2, 512], BF16)
    xc = p.sb("xc", [128, 3, TT], F32)
    xb = p.sb("xb", [128, 2, TT], BF16)
    xq = p.sb("xq", [128, 2, TT], BF16)
    mrow = p.sb("mrow", [1, TT], F32)
    vrow = p.sb("vrow", [1, TT], F32)
    srow = p.sb("srow", [1, TT], F32)
    arow = p.sb("arow", [1, TT], F32)
    brow = p.sb("brow", [1, TT], F32)
    tmpf = p.sb("tmpf", [128, TT], F32)
    tmp2 = p.sb("tmp2", [128, 2, TT], BF16)
    wlns = p.sb("wlns", [128, 16], F32)
    rprob = p.sb("rprob", [128, 9], F32)
    rlog = p.sb("rlog", [128, 9], F32)
    rpb = p.sb("rpb", [128, 9], BF16)
    rm8 = p.sb("rm8", [128, 8], F32)
    rsc = p.sb("rsc", [128, 4], F32)
    idnf = p.sb("idnf", [128, 128], F32)
    idn_b = p.sb("idn_b", [128, 128], BF16)
    ones_b = p.sb("ones_b", [128, 1], BF16)
    ones_r = p.sb("ones_r", [1, 128], F32)
    ones_rb = p.sb("ones_rb", [1, 128], BF16)
    wt = p.sb("wt", [128, 2, NCH * 128], BF16)
    crows = wt[0:9, 0, 0:TPC]
    wdt = p.sb("wdt", [128, 1, 44 * 128], BF16)
    obuf = p.sb("obuf", [128, 4, 512], BF16)
    xo = p.sb("xo", [128, 2, 1024], F32)

    ps_a = p.ps("ps_a", [128, 512])
    ps_b = p.ps("ps_b", [128, 512])
    ps_c = p.ps("ps_c", [128, 512])
    ps_d = p.ps("ps_d", [128, 512])
    ps_s1 = p.ps("ps_s1", [1, 512])
    ps_s2 = p.ps("ps_s2", [1, 512])
    ps_r1 = p.ps("ps_r1", [128, 512])
    ps_r2 = p.ps("ps_r2", [128, 512])
    ps_ab = [ps_a, ps_b]
    ps_cd = [ps_c, ps_d]

    s_pre = p.sem("s_pre")
    s_w = p.sem("s_w")
    s_w2 = p.sem("s_w2")
    s_pm = p.sem("s_pm")
    s_pm2 = p.sem("s_pm2")
    s_ev = p.sem("s_ev")
    s_ev2 = p.sem("s_ev2")
    s_od = p.sem("s_od")
    s_xc = p.sem("s_xc")
    s_cast = p.sem("s_cast")
    s_stmm = p.sem("s_stmm")
    s_vr = p.sem("s_vr")
    s_sq = p.sem("s_sq")
    s_row = p.sem("s_row")
    s_bc = p.sem("s_bc")
    s_ap = p.sem("s_ap")
    s_h2 = p.sem("s_h2")
    s_rt = p.sem("s_rt")

    _pre_cbs = [s_pre.inc(16) for _ in range(4)]

    def pre(g, cbs=_pre_cbs):
        g.memset(ones_b[:, :], 1.0)
        g.memset(ones_r[:, :], 1.0)
        g.memset(ones_rb[:, :], 1.0)
        k = 0
        for (dst, src) in [(wlns, wln2), (idnf, idf), (idn_b, idb)]:
            g.wait_ge(s_pre.h, 16 * k)
            cbs[k](g.dma_start(out=dst[:, :], in_=src[:, :]))
            k += 1
        g.wait_ge(s_pre.h, 16 * k)
        cbs[k](g.dma_start(out=act[:, :, :],
                           in_=ctxT.rearrange("(c p) t -> p c t", p=128)))
        k += 1
    p.on("gpsimd", pre)
    pre_n = s_pre.n

    wt_slots = Slots(2)
    wdt_slots = Slots(1)
    bank_slots = Slots(2)
    bank2_slots = Slots(2)
    ob_slots = Slots(4)
    ob32_slots = Slots(2)
    xc_slots = Slots(3)
    t2_slots = Slots(2)
    st = {"pm": 0, "pm2": 0, "ev": 0, "ev2": 0, "xc": 0, "t2": 0}

    def xload(src_hbm, f, tt, n=TT):
        xslot = st["xc"] % 3
        st["xc"] += 1
        awx = xc_slots.acquire(xslot)
        xprev = s_xc.n
        cbx = s_xc.inc(16)

        def op(g, f=f, tt=tt, xslot=xslot, awx=awx, cbx=cbx, xprev=xprev,
               src_hbm=src_hbm, n=n):
            awx(g)
            s_xc.wait(g, xprev)
            cbx(g.dma_start(
                out=xc[:, xslot, 0:n],
                in_=src_hbm[f * 128:(f + 1) * 128, tt * n:(tt + 1) * n]))
        p.on("gpsimd", op)
        return xslot, s_xc.n

    def wload16(w_hbm_blk, wbuf, slots, sem, ncols):
        wslot = slots.idx % 2 if hasattr(slots, "idx") else 0
        return None

    def stream_proj(w_hbm, nob, rhs_view, evac, first_wait=None, ob0=0):
        for ob in range(nob):
            wslot = (ob + ob0) % 2
            aw = wt_slots.acquire(wslot)
            wprev = s_w.n
            cbw = s_w.inc(16)
            wneed = s_w.n

            def wl(g, ob=ob, wslot=wslot, aw=aw, cbw=cbw, wprev=wprev,
                   w_hbm=w_hbm):
                aw(g)
                s_w.wait(g, wprev)
                cbw(g.dma_start(out=wt[:, wslot, :], in_=w_hbm[ob]))
            p.on("gpsimd", wl)
            for tt in range(4):
                bank = st["pm"] % 2
                st["pm"] += 1
                awb = bank_slots.acquire(bank)
                cbm = s_pm.inc(1)

                def mm(t, ob=ob, tt=tt, bank=bank, wslot=wslot, wneed=wneed,
                       awb=awb, cbm=cbm,
                       fw=(first_wait if (ob == 0 and tt == 0) else None)):
                    if fw is not None:
                        fw(t)
                    s_w.wait(t, wneed)
                    awb(t)
                    for cc in range(NCH):
                        m2 = t.matmul(ps_ab[bank][:, :],
                                      wt[:, wslot, cc * 128:(cc + 1) * 128],
                                      rhs_view(cc, tt),
                                      start=(cc == 0), stop=(cc == NCH - 1))
                    cbm(m2)
                p.on("tensor", mm)
                evac(ob, tt, bank, s_pm.n)
            wt_slots.release_at(wslot, s_pm)

    def rhs_ctx(cc, tt):
        return act[:, cc, tt * 512:(tt + 1) * 512]

    def evac_p1(ob, tt, bank, pmneed):
        xslot, xcn = xload(xh1T, ob, tt)
        oslot = st["ev2"] % 2
        st["ev2"] += 1
        awo = ob32_slots.acquire(oslot)
        cbe = s_ev.inc(1)

        def ev(v, bank=bank, oslot=oslot, xslot=xslot, pmneed=pmneed,
               xcn=xcn, awo=awo, cbe=cbe):
            s_pm.wait(v, pmneed)
            s_xc.wait(v, xcn)
            awo(v)
            cbe(v.tensor_tensor(xo[:, oslot, 0:512], ps_ab[bank][:, :],
                                xc[:, xslot, :], OP.add))
        p.on("vector", ev)
        bank_slots.release_at(bank, s_ev)
        xc_slots.release_at(xslot, s_ev)
        evneed = s_ev.n
        oprev = s_od.n
        cbo = s_od.inc(16)

        def od(g, ob=ob, tt=tt, oslot=oslot, evneed=evneed, cbo=cbo,
               oprev=oprev):
            s_ev.wait(g, evneed)
            s_od.wait(g, oprev)
            cbo(g.dma_start(
                out=xmT[ob * 128:(ob + 1) * 128, tt * TT:(tt + 1) * TT],
                in_=xo[:, oslot, 0:512]))
        p.on("sync", od)
        ob32_slots.release_at(oslot, s_od)

    def fw_pre(t):
        s_pre.wait(t, pre_n)
    p.on("tensor", lambda t: None)
    stream_proj(wo2, 16, rhs_ctx, evac_p1, first_wait=fw_pre)
    p1_od = s_od.n

    p.on("gpsimd", lambda g, v=p1_od: s_od.wait(g, v))
    for tt in range(4):
        for f in range(NCH):
            i = tt * NCH + f
            xslot, xcn = xload(xmT, f, tt)
            cb = s_cast.inc(1)

            def sq_op(sc, i=i, xslot=xslot, xcn=xcn, cb=cb):
                s_xc.wait(sc, xcn)
                if i >= 2:
                    s_stmm.wait(sc, i - 1)
                cb(sc.activation(xq[:, i % 2, :], xc[:, xslot, :], AF.Square))
            p.on("scalar", sq_op)
            xc_slots.release_at(xslot, s_cast)
            cb2 = s_stmm.inc(1)

            def st_mm(t, f=f, i=i, tt=tt, cb2=cb2):
                s_cast.wait(t, i + 1)
                if f == 0 and tt > 0:
                    s_row.wait(t, tt)
                cb2(t.matmul(ps_s2[:, :], ones_b[:, :], xq[:, i % 2, :],
                             start=(f == 0), stop=(f == NCH - 1)))
            p.on("tensor", st_mm)
        cbv2 = s_vr.inc(1)

        def rows_v(v, tt=tt, cbv2=cbv2):
            s_stmm.wait(v, (tt + 1) * NCH)
            s_sq.wait(v, tt)
            cbv2(v.tensor_scalar(vrow[:, :], ps_s2[:, :], 1.0 / D, RMS_EPS,
                                 OP.mult, OP.add))
        p.on("vector", rows_v)
        cbs2 = s_sq.inc(1)

        def rows_s(sc, tt=tt, cbs2=cbs2):
            s_vr.wait(sc, tt + 1)
            s_bc.wait(sc, tt)
            cbs2(sc.activation(arow[:, :], vrow[:, :], AF.Sqrt))
        p.on("scalar", rows_s)
        cbr = s_row.inc(1)

        def rows_r(v, tt=tt, cbr=cbr):
            s_sq.wait(v, tt + 1)
            v.reciprocal(arow[:, :], arow[:, :])
            v.drain()
            cbr(v.tensor_scalar(arow[:, :], arow[:, :], 1.0, None, OP.mult))
        p.on("vector", rows_r)
        cbb = s_bc.inc(1)

        def bc_op(t, tt=tt, cbb=cbb):
            s_row.wait(t, tt + 1)
            if tt > 0:
                s_ap.wait(t, tt * NCH)
            cbb(t.matmul(ps_r1[:, :], ones_r[:, :], arow[:, :], start=True,
                         stop=True, skip_group_check=True))
        p.on("tensor", bc_op)
        for f in range(NCH):
            xslot, xcn = xload(xmT, f, tt)
            t2s = st["t2"] % 2
            st["t2"] += 1
            awt2 = t2_slots.acquire(t2s)
            cba = s_ap.inc(1)

            def ap_op(v, f=f, tt=tt, xslot=xslot, xcn=xcn, t2s=t2s,
                      awt2=awt2, cba=cba):
                s_bc.wait(v, tt + 1)
                s_xc.wait(v, xcn)
                awt2(v)
                cba(v.scalar_tensor_tensor(
                    tmp2[:, t2s, :], xc[:, xslot, :], wlns[:, f:f + 1],
                    ps_r1[:, :], OP.mult, OP.mult))
            p.on("vector", ap_op)
            xc_slots.release_at(xslot, s_ap)
            apn = s_ap.n
            oprev = s_od.n
            cbo = s_od.inc(16)

            def h2out(g, f=f, tt=tt, t2s=t2s, cbo=cbo, apn=apn, oprev=oprev):
                s_ap.wait(g, apn)
                s_od.wait(g, oprev)
                cbo(g.dma_start(
                    out=h2T[f * 128:(f + 1) * 128, tt * TT:(tt + 1) * TT],
                    in_=tmp2[:, t2s, :]))
            p.on("sync", h2out)
            t2_slots.release_at(t2s, s_od)
    p2_od = s_od.n

    h2c = wdt[:, 0, 0:2 * NCH * 128].rearrange("p (s c) -> p s c", s=2)
    rmsk = p.sb("rmsk", [128, 8], F32)
    h2c_slots = Slots(2)
    s_h2c = p.sem("s_h2c")
    s_rtm = p.sem("s_rtm")
    s_rte = p.sem("s_rte")
    s_rtt = p.sem("s_rtt")
    ps_rt = ps_c[:, 0:16]
    ps_tr = ps_d[0:9, 0:128]
    wrts = p.sb("wrts", [128, NCH * 16], BF16)
    _c = s_pre.inc(16)

    def wrtl(g, cb=_c, prev=s_pre.n - 16):
        s_pre.wait(g, prev)
        cb(g.dma_start(out=wrts.rearrange("p (c n) -> p c n", c=NCH),
                       in_=wrt9.rearrange("c p n -> p c n")))
    p.on("gpsimd", wrtl)
    wrt_n = s_pre.n
    p.on("gpsimd", lambda g, v=p2_od: s_od.wait(g, v))

    for tk in range(16):
        hslot = tk % 2
        awh = h2c_slots.acquire(hslot)
        hprev = s_h2c.n
        cbh = s_h2c.inc(16)

        def h2l(g, tk=tk, hslot=hslot, awh=awh, cbh=cbh, hprev=hprev):
            awh(g)
            s_h2c.wait(g, hprev)
            cbh(g.dma_start(
                out=h2c[:, hslot, :].rearrange("p (c t) -> p c t", c=NCH),
                in_=h2T[:, tk * 128:(tk + 1) * 128].rearrange(
                    "(c p) t -> p c t", p=128)))
        p.on("gpsimd", h2l)
        hn = s_h2c.n
        cbm = s_rtm.inc(1)

        def rtm(t, tk=tk, hslot=hslot, hn=hn, cbm=cbm):
            if tk == 0:
                s_pre.wait(t, wrt_n)
            else:
                s_rte.wait(t, 3 * tk - 1)
            s_h2c.wait(t, hn)
            for cc in range(NCH):
                mm = t.matmul(ps_rt[:, :],
                              h2c[:, hslot, cc * 128:(cc + 1) * 128],
                              wrts[:, cc * 16:(cc + 1) * 16], start=(cc == 0),
                              stop=(cc == NCH - 1), skip_group_check=True)
            cbm(mm)
        p.on("tensor", rtm)
        h2c_slots.release_at(hslot, s_rtm)
        rtmn = s_rtm.n

        cbe = s_rte.inc(1)

        def rte_a(v, rtmn=rtmn, cbe=cbe):
            def fl():
                v.tensor_scalar(tmpf[:, 0:256], tmpf[:, 0:256], 0.0, None,
                                OP.add)
            s_rtm.wait(v, rtmn)
            v.tensor_copy(rlog[:, :], ps_rt[:, 0:9])
            fl()
            v.tensor_tensor(rm8[:, 0:4], rlog[:, 0:4], rlog[:, 4:8], OP.max)
            fl()
            v.tensor_tensor(rm8[:, 4:6], rm8[:, 0:2], rm8[:, 2:4], OP.max)
            fl()
            v.tensor_tensor(rm8[:, 6:7], rm8[:, 4:5], rm8[:, 5:6], OP.max)
            fl()
            cbe(v.tensor_scalar(rsc[:, 0:1], rm8[:, 6:7], -1.0, None,
                                OP.mult))
        p.on("vector", rte_a)
        cbs3 = s_rte.inc(1)

        def rts(sc, tk=tk, need=s_rte.n - 1, cbs3=cbs3):
            s_rte.wait(sc, need)
            if tk > 0:
                s_rtt.wait(sc, 2 * tk - 1)
            sc.activation(rprob[:, 0:8], rlog[:, 0:8], AF.Exp,
                          bias=rsc[:, 0:1])
            cbs3(sc.activation(rprob[:, 8:9], rlog[:, 8:9], AF.Sigmoid))
        p.on("scalar", rts)
        cbe2 = s_rte.inc(1)

        def rte_b(v, need=s_rte.n - 1, cbe2=cbe2):
            def fl():
                v.tensor_scalar(tmpf[:, 0:256], tmpf[:, 0:256], 0.0, None,
                                OP.add)
            s_rte.wait(v, need)
            v.tensor_tensor(rsc[:, 0:4], rprob[:, 0:4], rprob[:, 4:8], OP.add)
            fl()
            v.tensor_tensor(rsc[:, 0:2], rsc[:, 0:2], rsc[:, 2:4], OP.add)
            fl()
            v.tensor_tensor(rsc[:, 0:1], rsc[:, 0:1], rsc[:, 1:2], OP.add)
            fl()
            v.reciprocal(rsc[:, 0:1], rsc[:, 0:1])
            fl()
            v.tensor_scalar(rprob[:, 0:8], rprob[:, 0:8], rsc[:, 0:1], None,
                            OP.mult)
            fl()
            v.tensor_tensor(rm8[:, 0:4], rprob[:, 0:4], rprob[:, 4:8], OP.max)
            fl()
            v.tensor_tensor(rm8[:, 4:6], rm8[:, 0:2], rm8[:, 2:4], OP.max)
            fl()
            v.tensor_tensor(rm8[:, 6:7], rm8[:, 4:5], rm8[:, 5:6], OP.max)
            fl()
            v.tensor_scalar(rmsk[:, :], rprob[:, 0:8], rm8[:, 6:7], None,
                            OP.is_lt)
            fl()
            v.tensor_tensor(rmsk[:, :], rprob[:, 0:8], rmsk[:, :], OP.mult)
            fl()
            v.tensor_tensor(rm8[:, 0:4], rmsk[:, 0:4], rmsk[:, 4:8], OP.max)
            fl()
            v.tensor_tensor(rm8[:, 4:6], rm8[:, 0:2], rm8[:, 2:4], OP.max)
            fl()
            v.tensor_tensor(rm8[:, 7:8], rm8[:, 4:5], rm8[:, 5:6], OP.max)
            fl()
            v.tensor_scalar(rmsk[:, :], rprob[:, 0:8], rm8[:, 7:8], None,
                            OP.is_ge)
            fl()
            v.tensor_tensor(rprob[:, 0:8], rprob[:, 0:8], rmsk[:, :],
                            OP.mult)
            fl()
            cbe2(v.tensor_copy(rpb[:, :], rprob[:, :]))
        p.on("vector", rte_b)
        rten = s_rte.n
        cbt = s_rtt.inc(1)

        def rtt(t, tk=tk, rten=rten, cbt=cbt):
            s_rte.wait(t, rten)
            if tk > 0:
                s_rtt.wait(t, 2 * tk)
            cbt(t.transpose(ps_tr[:, :], rprob[:, :], idnf[:, :]))
        p.on("tensor", rtt)
        cbv3 = s_rtt.inc(1)

        def rtv(v, tk=tk, need=s_rtt.n - 1, cbv3=cbv3):
            s_rtt.wait(v, need)
            cbv3(v.tensor_copy(crows[:, tk * 128:(tk + 1) * 128],
                               ps_tr[:, :]))
        p.on("vector", rtv)
    p3_rtt = s_rtt.n
    wdt_slots.req[0] = (s_rtm, 16)

    crow1 = p.sb("crow1", [1, 1024], BF16)
    s_cr = p.sem("s_cr")
    s_crh = p.sem("s_crh")
    cbdma = s_crh.inc(16)

    def crows_out(g, cb=cbdma, v=p3_rtt):
        s_rtt.wait(g, v)
        cb(g.dma_start(out=crowsH[:, :], in_=crows[:, :]))
    p.on("gpsimd", crows_out)
    wt_slots.req[0] = (s_crh, 16)
    wt_slots.req[1] = (s_crh, 16)
    s_cb = p.sem("s_cb")
    s_cbe = p.sem("s_cbe")
    s_gu = p.sem("s_gu")
    s_am = p.sem("s_am")
    s_dn = p.sem("s_dn")
    s_ra = p.sem("s_ra")
    s_x3 = p.sem("s_x3")
    gs_slots = Slots(2)
    us_slots = Slots(2)
    xo_slots = Slots(2)

    a_moe = big[:, 0:11 * 1024].rearrange("p (c t) -> p c t", c=11, t=1024)
    a_sh = big[:, 0:44 * 512].rearrange("p (c t) -> p c t", c=44, t=512)

    x3_od_mark = []
    for h in range(2):
        t0 = h * 1024
        hprev = s_h2c.n
        cbh = s_h2c.inc(16)
        ran = s_ra.n

        def h2lh(g, h=h, t0=t0, cbh=cbh, hprev=hprev, ran=ran):
            s_h2c.wait(g, hprev)
            if h > 0:
                s_ra.wait(g, ran)
            cbh(g.dma_start(
                out=h2h[:, :, :],
                in_=h2T[:, t0:t0 + 1024].rearrange("(c p) t -> p c t",
                                                   p=128)))
        p.on("gpsimd", h2lh)
        h2n = s_h2c.n

        elist = list(range(9))
        if "noshared" in dbg:
            elist = list(range(8))
        if "nomoe" in dbg:
            elist = [0, 8]
        if "onlye0" in dbg:
            elist = [0]
        for e in elist:
            shared = (e == 8)
            ni = 44 if shared else 11
            wge = wsg if shared else wg[e]
            wue = wsu if shared else wu[e]
            wde = wsd if shared else wd[e]
            crprev = s_cr.n
            cbcr = s_cr.inc(16)

            def crl(g, e=e, h=h, cbcr=cbcr, crprev=crprev, cbn=s_cb.n):
                s_crh.wait(g, 16)
                s_cr.wait(g, crprev)
                if cbn > 0:
                    s_cb.wait(g, cbn)
                cbcr(g.dma_start(out=crow1[:, :],
                                 in_=crowsH[e:e + 1,
                                            h * 1024:(h + 1) * 1024]))
            p.on("gpsimd", crl)
            crn = s_cr.n
            cbm1 = s_cb.inc(1)
            cbm2 = s_cb.inc(1)

            def cbop(t, cbm1=cbm1, cbm2=cbm2, crn=crn, gun=s_gu.n):
                s_cr.wait(t, crn)
                if gun > 0:
                    s_gu.wait(t, gun)
                cbm1(t.matmul(ps_r1[:, :], ones_rb[:, :], crow1[:, 0:512],
                              start=True, stop=True, skip_group_check=True))
                cbm2(t.matmul(ps_r2[:, :], ones_rb[:, :], crow1[:, 512:1024],
                              start=True, stop=True, skip_group_check=True))
            p.on("tensor", cbop)
            cbmn = s_cb.n
            cbe1 = s_cbe.inc(1)

            def cbev(v, cbe1=cbe1, cbmn=cbmn):
                s_cb.wait(v, cbmn)
                v.tensor_copy(cbs_sb[:, 0:512], ps_r1[:, :])
                cbe1(v.tensor_copy(cbs_sb[:, 512:1024], ps_r2[:, :]))
            p.on("vector", cbev)
            cben = s_cbe.n

            def gu_chunk(i, q, wge, wue, dst_ap):
                aw = wt_slots.acquire(0)
                wprev = s_w.n
                cbw = s_w.inc(16)
                wneed = s_w.n

                def wlg(g, aw=aw, cbw=cbw, wprev=wprev, wge=wge, i=i):
                    aw(g)
                    s_w.wait(g, wprev)
                    cbw(g.dma_start(out=wt[:, 0, :], in_=wge[i]))
                p.on("gpsimd", wlg)
                aw2 = wt_slots.acquire(1)
                wprev2 = s_w.n
                cbw2 = s_w.inc(16)
                wneed2 = s_w.n

                def wlu(g, aw2=aw2, cbw2=cbw2, wprev2=wprev2, wue=wue, i=i):
                    aw2(g)
                    s_w.wait(g, wprev2)
                    cbw2(g.dma_start(out=wt[:, 1, :], in_=wue[i]))
                p.on("gpsimd", wlu)

                bank = st["pm"] % 2
                st["pm"] += 1
                awb = bank_slots.acquire(bank)
                cbg = s_gu.inc(1)

                def gmm(t, q=q, bank=bank, wneed=wneed, awb=awb, cbg=cbg,
                        h2n=h2n):
                    s_h2c.wait(t, h2n)
                    s_w.wait(t, wneed)
                    awb(t)
                    for cc in range(NCH):
                        mm = t.matmul(ps_ab[bank][:, :],
                                      wt[:, 0, cc * 128:(cc + 1) * 128],
                                      h2h[:, cc, q * 512:(q + 1) * 512],
                                      start=(cc == 0), stop=(cc == NCH - 1),
                                      skip_group_check=True)
                    cbg(mm)
                p.on("tensor", gmm)
                wt_slots.release_at(0, s_gu)
                gmn = s_gu.n
                awg = gs_slots.acquire(0)
                cbsg = s_gu.inc(1)

                def gev(sc, bank=bank, gmn=gmn, awg=awg, cbsg=cbsg):
                    s_gu.wait(sc, gmn)
                    awg(sc)
                    cbsg(sc.activation(gs[:, 0, :], ps_ab[bank][:, :],
                                       AF.Silu))
                p.on("scalar", gev)
                bank_slots.release_at(bank, s_gu)

                bank2 = st["pm"] % 2
                st["pm"] += 1
                awb2 = bank_slots.acquire(bank2)
                cbu = s_gu.inc(1)

                def umm(t, q=q, bank2=bank2, wneed2=wneed2, awb2=awb2,
                        cbu=cbu):
                    s_w.wait(t, wneed2)
                    awb2(t)
                    for cc in range(NCH):
                        mm = t.matmul(ps_ab[bank2][:, :],
                                      wt[:, 1, cc * 128:(cc + 1) * 128],
                                      h2h[:, cc, q * 512:(q + 1) * 512],
                                      start=(cc == 0), stop=(cc == NCH - 1),
                                      skip_group_check=True)
                    cbu(mm)
                p.on("tensor", umm)
                wt_slots.release_at(1, s_gu)
                umn = s_gu.n
                awu = us_slots.acquire(0)
                cbue = s_gu.inc(1)

                def uev(v, q=q, bank2=bank2, umn=umn, awu=awu, cbue=cbue,
                        cben=cben):
                    s_gu.wait(v, umn)
                    s_cbe.wait(v, cben)
                    awu(v)
                    cbue(v.tensor_tensor(us[:, 0, :], ps_ab[bank2][:, :],
                                         cbs_sb[:, q * 512:(q + 1) * 512],
                                         OP.mult))
                p.on("vector", uev)
                bank_slots.release_at(bank2, s_gu)
                guns = s_gu.n
                dnn = s_dn.n
                cbam = s_am.inc(1)

                def amul(v, guns=guns, dnn=dnn, cbam=cbam, dst_ap=dst_ap):
                    s_gu.wait(v, guns)
                    if dnn > 0:
                        s_dn.wait(v, dnn)
                    cbam(v.tensor_tensor(dst_ap, gs[:, 0, :], us[:, 0, :],
                                         OP.mult))
                p.on("vector", amul)
                gs_slots.release_at(0, s_am)
                us_slots.release_at(0, s_am)

            def down_block(f, q, ni, rhs_fn, wde, first_e):
                aw = wdt_slots.acquire(0)
                wprev = s_w2.n
                cbw = s_w2.inc(16)
                wneed = s_w2.n

                def wld(g, f=f, aw=aw, cbw=cbw, wprev=wprev, wde=wde, ni=ni):
                    aw(g)
                    s_w2.wait(g, wprev)
                    cbw(g.dma_start(out=wdt[:, 0, 0:ni * 128], in_=wde[f]))
                p.on("gpsimd", wld)
                bank = st["pm2"] % 2
                st["pm2"] += 1
                awb = bank2_slots.acquire(bank)
                cbd = s_dn.inc(1)
                amn = s_am.n

                def dmm(t, f=f, q=q, bank=bank, wneed=wneed, awb=awb,
                        cbd=cbd, amn=amn, ni=ni, rhs_fn=rhs_fn):
                    s_am.wait(t, amn)
                    s_w2.wait(t, wneed)
                    awb(t)
                    for cc in range(ni):
                        mm = t.matmul(ps_cd[bank][:, :],
                                      wdt[:, 0, cc * 128:(cc + 1) * 128],
                                      rhs_fn(cc),
                                      start=(cc == 0), stop=(cc == ni - 1),
                                      skip_group_check=True)
                    cbd(mm)
                p.on("tensor", dmm)
                dnb = s_dn.n
                cbra = s_ra.inc(1)

                def raev(v, q=q, f=f, h=h, bank=bank, dnb=dnb, cbra=cbra,
                         first_e=first_e, x3n=(32 * h)):
                    s_dn.wait(v, dnb)
                    if first_e and h > 0:
                        s_x3.wait(v, x3n)
                    if first_e:
                        cbra(v.tensor_copy(
                            routed[:, f, q * 512:(q + 1) * 512],
                            ps_cd[bank][:, :]))
                    else:
                        cbra(v.tensor_tensor(
                            routed[:, f, q * 512:(q + 1) * 512],
                            routed[:, f, q * 512:(q + 1) * 512],
                            ps_cd[bank][:, :], OP.add))
                p.on("vector", raev)
                bank2_slots.release_at(bank, s_ra)
                wdt_slots.release_at(0, s_dn)

            if not shared:
                for i in range(ni):
                    for q in range(2):
                        gu_chunk(i, q, wge, wue,
                                 a_moe[:, i, q * 512:(q + 1) * 512])
                for f in range(16):
                    for q in range(2):
                        down_block(
                            f, q, ni,
                            (lambda cc, q=q: a_moe[:, cc,
                                                   q * 512:(q + 1) * 512]),
                            wde, e == 0)
            else:
                for q in range(2):
                    for i in range(ni):
                        gu_chunk(i, q, wge, wue, a_sh[:, i, :])
                    for f in range(16):
                        down_block(f, q, ni,
                                   (lambda cc: a_sh[:, cc, :]), wde, False)

        ra_half = s_ra.n
        for f in range(16):
            for q in range(2):
                tq = 2 * h + q
                xs1, xn1 = xload(x1T, f, tq)
                xs2, xn2 = xload(xmT, f, tq)
                oslot = st["t2"] % 2
                st["t2"] += 1
                awo = xo_slots.acquire(oslot)
                cbx3 = s_x3.inc(1)

                def x3op(v, f=f, q=q, xs1=xs1, xn1=xn1, xs2=xs2, xn2=xn2,
                         oslot=oslot, awo=awo, cbx3=cbx3, ra_half=ra_half):
                    s_ra.wait(v, ra_half)
                    s_xc.wait(v, xn2)
                    awo(v)
                    v.tensor_tensor(xo[:, oslot, 0:512],
                                    routed[:, f, q * 512:(q + 1) * 512],
                                    xc[:, xs1, :], OP.add)
                    cbx3(v.tensor_tensor(xo[:, oslot, 0:512],
                                         xo[:, oslot, 0:512],
                                         xc[:, xs2, :], OP.add))
                p.on("vector", x3op)
                xc_slots.release_at(xs1, s_x3)
                xc_slots.release_at(xs2, s_x3)
                x3n = s_x3.n
                oprev = s_od.n
                cbo = s_od.inc(16)

                def x3out(g, f=f, tq=tq, oslot=oslot, x3n=x3n, oprev=oprev,
                          cbo=cbo):
                    s_x3.wait(g, x3n)
                    s_od.wait(g, oprev)
                    cbo(g.dma_start(
                        out=x3T[f * 128:(f + 1) * 128,
                                tq * 512:(tq + 1) * 512],
                        in_=xo[:, oslot, 0:512]))
                p.on("sync", x3out)
                xo_slots.release_at(oslot, s_od)
    p4_od = s_od.n
    p4_dn = s_dn.n

    s_cast5 = p.sem("s_cast5")
    s_stm5 = p.sem("s_stm5")
    s_vr5 = p.sem("s_vr5")
    s_sq5 = p.sem("s_sq5")
    s_row5 = p.sem("s_row5")
    s_bc5 = p.sem("s_bc5")
    s_ap5 = p.sem("s_ap5")
    act16 = big[:, 0:NCH * 2048].rearrange("p (c t) -> p c t", c=NCH, t=2048)

    p.on("gpsimd", lambda g, v=p4_od: s_od.wait(g, v))
    for tt in range(4):
        for f in range(NCH):
            i = tt * NCH + f
            xslot, xcn = xload(x3T, f, tt)
            cb = s_cast5.inc(1)

            def cast5(sc, i=i, xslot=xslot, xcn=xcn, cb=cb):
                s_xc.wait(sc, xcn)
                if i >= 2:
                    s_stm5.wait(sc, i - 1)
                sc.copy(xb[:, i % 2, :], xc[:, xslot, :])
                cb(sc.activation(xq[:, i % 2, :], xc[:, xslot, :], AF.Square))
            p.on("scalar", cast5)
            xc_slots.release_at(xslot, s_cast5)
            cb2 = s_stm5.inc(1)

            def stm5(t, f=f, i=i, tt=tt, cb2=cb2):
                s_cast5.wait(t, i + 1)
                if f == 0 and tt == 0:
                    s_dn.wait(t, p4_dn)
                if f == 0 and tt > 0:
                    s_row5.wait(t, tt)
                t.matmul(ps_s1[:, :], ones_b[:, :], xb[:, i % 2, :],
                         start=(f == 0), stop=(f == NCH - 1))
                cb2(t.matmul(ps_s2[:, :], ones_b[:, :], xq[:, i % 2, :],
                             start=(f == 0), stop=(f == NCH - 1)))
            p.on("tensor", stm5)
        cbv = s_vr5.inc(1)

        def rows5a(v, tt=tt, cbv=cbv):
            s_stm5.wait(v, (tt + 1) * NCH)
            s_sq5.wait(v, tt)
            v.tensor_scalar(mrow[:, :], ps_s1[:, :], 1.0 / D, None, OP.mult)
            v.tensor_tensor(srow[:, :], mrow[:, :], mrow[:, :], OP.mult)
            v.tensor_scalar(vrow[:, :], ps_s2[:, :], 1.0 / D, LN_EPS,
                            OP.mult, OP.add)
            cbv(v.tensor_tensor(vrow[:, :], vrow[:, :], srow[:, :],
                                OP.subtract))
        p.on("vector", rows5a)
        cbs5 = s_sq5.inc(1)

        def rows5b(sc, tt=tt, cbs5=cbs5):
            s_vr5.wait(sc, tt + 1)
            s_bc5.wait(sc, tt)
            cbs5(sc.activation(arow[:, :], vrow[:, :], AF.Sqrt))
        p.on("scalar", rows5b)
        cbr = s_row5.inc(1)

        def rows5c(v, tt=tt, cbr=cbr):
            s_sq5.wait(v, tt + 1)
            v.reciprocal(arow[:, :], arow[:, :])
            v.drain()
            cbr(v.scalar_tensor_tensor(brow[:, :], mrow[:, :], -1.0,
                                       arow[:, :], OP.mult, OP.mult))
        p.on("vector", rows5c)
        cbb = s_bc5.inc(1)

        def bc5(t, tt=tt, cbb=cbb):
            s_row5.wait(t, tt + 1)
            if tt > 0:
                s_ap5.wait(t, tt * NCH)
            t.matmul(ps_r1[:, :], ones_r[:, :], arow[:, :], start=True,
                     stop=True, skip_group_check=True)
            cbb(t.matmul(ps_r2[:, :], ones_r[:, :], brow[:, :], start=True,
                         stop=True, skip_group_check=True))
        p.on("tensor", bc5)
        for f in range(NCH):
            xslot, xcn = xload(x3T, f, tt)
            cba = s_ap5.inc(1)

            def ap5(v, f=f, tt=tt, xslot=xslot, xcn=xcn, cba=cba):
                s_bc5.wait(v, tt + 1)
                s_x3.wait(v, 64)
                s_xc.wait(v, xcn)
                v.tensor_tensor(tmpf[:, :], xc[:, xslot, :], ps_r1[:, :],
                                OP.mult)
                cba(v.tensor_tensor(act16[:, f, tt * 512:(tt + 1) * 512],
                                    tmpf[:, :], ps_r2[:, :], OP.add))
            p.on("vector", ap5)
            xc_slots.release_at(xslot, s_ap5)
    p5_ap = s_ap5.n

    def rhs_act(cc, tt):
        return act16[:, cc, tt * 512:(tt + 1) * 512]

    def mk_evac(out_hbm):
        def evac(ob_local, tt, bank, pmneed, ob_off=0, out_hbm=out_hbm):
            oslot = st["ev"] % 4
            st["ev"] += 1
            awo = ob_slots.acquire(oslot)
            cbe = s_ev.inc(1)

            def ev(v, bank=bank, oslot=oslot, pmneed=pmneed, awo=awo,
                   cbe=cbe):
                s_pm.wait(v, pmneed)
                awo(v)
                cbe(v.tensor_copy(obuf[:, oslot, :], ps_ab[bank][:, :]))
            p.on("vector", ev)
            bank_slots.release_at(bank, s_ev)
            evneed = s_ev.n
            oprev = s_od.n
            cbo = s_od.inc(16)

            def od(g, ob=ob_local, tt=tt, oslot=oslot, evneed=evneed,
                   cbo=cbo, oprev=oprev, out_hbm=out_hbm):
                s_ev.wait(g, evneed)
                s_od.wait(g, oprev)
                cbo(g.dma_start(
                    out=out_hbm[ob * 128:(ob + 1) * 128,
                                tt * 512:(tt + 1) * 512],
                    in_=obuf[:, oslot, :]))
            p.on("sync", od)
            ob_slots.release_at(oslot, s_od)
        return evac

    def fw_p5(t):
        s_ap5.wait(t, p5_ap)

    class _W:
        def __init__(self, base, off):
            self.base = base
            self.off = off

        def __getitem__(self, i):
            return self.base[self.off + i]

    stream_proj(_W(wpq, 0), 16, rhs_act, mk_evac(q2t), first_wait=fw_p5)
    stream_proj(_W(wpq, 16), 16, rhs_act, mk_evac(k2t))

    wmv2 = wdt[:, 0:1, 0:NCH * 256]
    wmv2_slots = Slots(1)
    wmv2_slots.req[0] = (s_dn, p4_dn)
    for oc in range(8):
        aw = wmv2_slots.acquire(0)
        wprev = s_w.n
        cbw = s_w.inc(16)
        wneed = s_w.n

        def wvl(g, oc=oc, aw=aw, cbw=cbw, wprev=wprev):
            aw(g)
            s_w.wait(g, wprev)
            cbw(g.dma_start(out=wmv2[:, 0, :], in_=wpv[oc]))
        p.on("gpsimd", wvl)
        for tk in range(16):
            bank = st["pm"] % 2
            st["pm"] += 1
            awb = bank_slots.acquire(bank)
            cbm = s_pm.inc(1)

            def vmm(t, tk=tk, bank=bank, wneed=wneed, awb=awb, cbm=cbm):
                s_w.wait(t, wneed)
                awb(t)
                for cc in range(NCH):
                    mm = t.matmul(ps_ab[bank][:, 0:256],
                                  act16[:, cc, tk * 128:(tk + 1) * 128],
                                  wmv2[:, 0, cc * 256:(cc + 1) * 256],
                                  start=(cc == 0), stop=(cc == NCH - 1),
                                  skip_group_check=True)
                cbm(mm)
            p.on("tensor", vmm)
            pmneed = s_pm.n
            oslot = st["ev"] % 4
            st["ev"] += 1
            awo = ob_slots.acquire(oslot)
            cbe = s_ev.inc(1)

            def vev(v, bank=bank, oslot=oslot, pmneed=pmneed, awo=awo,
                    cbe=cbe):
                s_pm.wait(v, pmneed)
                awo(v)
                cbe(v.tensor_copy(obuf[:, oslot, 0:256],
                                  ps_ab[bank][:, 0:256]))
            p.on("vector", vev)
            bank_slots.release_at(bank, s_ev)
            evneed = s_ev.n
            oprev = s_od.n
            cbo = s_od.inc(16)

            def vod(g, oc=oc, tk=tk, oslot=oslot, evneed=evneed, cbo=cbo,
                    oprev=oprev):
                s_ev.wait(g, evneed)
                s_od.wait(g, oprev)
                cbo(g.dma_start(
                    out=v2[tk * 128:(tk + 1) * 128, oc * 256:(oc + 1) * 256],
                    in_=obuf[:, oslot, 0:256]))
            p.on("sync", vod)
            ob_slots.release_at(oslot, s_od)
        wmv2_slots.release_at(0, s_pm)

    def fin(g, v=s_od.n):
        s_od.wait(g, v)
    p.on("gpsimd", fin)
    return p.finish()



def build_launch_G():
    p = Prog("G")
    a2T = p.dram("a2T", [D, TPC], BF16)
    x3T = p.dram("x3T", [D, TPC], F32)
    wout = p.dram("wout", [16, 128, NCH * 128], F32)
    wgu_p = p.dram("wgu_p", [88, 128, NCH * 128], BF16)
    wdn_p = p.dram("wdn_p", [16, 128, 44 * 128], BF16)
    x4T = p.dram("x4T", [D, TPC], F32, "Internal")
    outT = p.dram("outT", [D, TPC], F32, "ExternalOutput")

    TT = 512
    act = p.sb("act", [128, NCH, TPC], BF16)
    hgu = p.sb("hgu", [128, 44, TT], BF16)
    yb = p.sb("yb", [128, NCH, TT], BF16)
    xc = p.sb("xc", [128, 3, TT], F32)
    xb = p.sb("xb", [128, 2, TT], BF16)
    xq = p.sb("xq", [128, 2, TT], BF16)
    mrow = p.sb("mrow", [1, TT], F32)
    vrow = p.sb("vrow", [1, TT], F32)
    srow = p.sb("srow", [1, TT], F32)
    arow = p.sb("arow", [1, TT], F32)
    brow = p.sb("brow", [1, TT], F32)
    tmpf = p.sb("tmpf", [128, TT], F32)
    gsb = p.sb("gsb", [128, 2, TT], BF16)
    usb = p.sb("usb", [128, 2, TT], BF16)
    ones_b = p.sb("ones_b", [128, 1], BF16)
    ones_r = p.sb("ones_r", [1, 128], F32)
    wt = p.sb("wt", [128, 2, NCH * 128], BF16)
    wtu = p.sb("wtu", [128, 2, NCH * 128], BF16)
    wdt = p.sb("wdt", [128, 1, 44 * 128], BF16)
    obuf = p.sb("obuf", [128, 2, 512], F32)

    ps_a = p.ps("ps_a", [128, 512])
    ps_b = p.ps("ps_b", [128, 512])
    ps_c = p.ps("ps_c", [128, 512])
    ps_d = p.ps("ps_d", [128, 512])
    ps_s1 = p.ps("ps_s1", [1, 512])
    ps_s2 = p.ps("ps_s2", [1, 512])
    ps_r1 = p.ps("ps_r1", [128, 512])
    ps_r2 = p.ps("ps_r2", [128, 512])
    ps_ab = [ps_a, ps_b]
    ps_cd = [ps_c, ps_d]

    s_pre = p.sem("s_pre")
    s_w = p.sem("s_w")
    s_w2 = p.sem("s_w2")
    s_pm = p.sem("s_pm")
    s_gu = p.sem("s_gu")
    s_hg = p.sem("s_hg")
    s_dn = p.sem("s_dn")
    s_ye = p.sem("s_ye")
    s_ev = p.sem("s_ev")
    s_od = p.sem("s_od")
    s_xc = p.sem("s_xc")
    s_cast = p.sem("s_cast")
    s_stmm = p.sem("s_stmm")
    s_vr = p.sem("s_vr")
    s_sq = p.sem("s_sq")
    s_row = p.sem("s_row")
    s_bc = p.sem("s_bc")
    s_ap = p.sem("s_ap")
    s_fo = p.sem("s_fo")

    _pre_cbs = [s_pre.inc(16)]

    def pre(g, cbs=_pre_cbs):
        g.memset(ones_b[:, :], 1.0)
        g.memset(ones_r[:, :], 1.0)
        cbs[0](g.dma_start(out=act[:, :, :],
                           in_=a2T.rearrange("(c p) t -> p c t", p=128)))
    p.on("gpsimd", pre)
    pre_n = s_pre.n

    wt_slots = Slots(2)
    wtu_slots = Slots(2)
    wdt_slots = Slots(1)
    bank_slots = Slots(2)
    bank2_slots = Slots(2)
    ob_slots = Slots(2)
    gs_slots = Slots(2)
    us_slots = Slots(2)
    xc_slots = Slots(3)
    st = {"pm": 0, "pm2": 0, "ev": 0, "xc": 0}

    def xload(src_hbm, f, tt):
        xslot = st["xc"] % 3
        st["xc"] += 1
        awx = xc_slots.acquire(xslot)
        xprev = s_xc.n
        cbx = s_xc.inc(16)

        def op(g, f=f, tt=tt, xslot=xslot, awx=awx, cbx=cbx, xprev=xprev,
               src_hbm=src_hbm):
            awx(g)
            s_xc.wait(g, xprev)
            cbx(g.dma_start(
                out=xc[:, xslot, :],
                in_=src_hbm[f * 128:(f + 1) * 128, tt * TT:(tt + 1) * TT]))
        p.on("gpsimd", op)
        return xslot, s_xc.n

    for ob in range(16):
        wslot = ob % 2
        aw = wt_slots.acquire(wslot)
        wprev = s_w.n
        cbw = s_w.inc(16)
        wneed = s_w.n

        def wload(g, ob=ob, wslot=wslot, aw=aw, cbw=cbw, wprev=wprev):
            aw(g)
            s_w.wait(g, wprev)
            cbw(g.dma_start(out=wt[:, wslot, :], in_=wout[ob]))
        p.on("gpsimd", wload)
        for tt in range(4):
            xslot, xcn = xload(x3T, ob, tt)
            bank = st["pm"] % 2
            st["pm"] += 1
            awb = bank_slots.acquire(bank)
            cbm = s_pm.inc(1)

            def mmop(t, tt=tt, bank=bank, wslot=wslot, wneed=wneed, awb=awb,
                     cbm=cbm, first=(ob == 0 and tt == 0), pre_n=pre_n):
                if first:
                    s_pre.wait(t, pre_n)
                s_w.wait(t, wneed)
                awb(t)
                for cc in range(NCH):
                    mm = t.matmul(ps_ab[bank][:, :],
                                  wt[:, wslot, cc * 128:(cc + 1) * 128],
                                  act[:, cc, tt * 512:(tt + 1) * 512],
                                  start=(cc == 0), stop=(cc == NCH - 1))
                cbm(mm)
            p.on("tensor", mmop)
            pmneed = s_pm.n
            oslot = st["ev"] % 2
            st["ev"] += 1
            awo = ob_slots.acquire(oslot)
            cbe = s_ev.inc(1)

            def evop(v, bank=bank, oslot=oslot, xslot=xslot, pmneed=pmneed,
                     xcn=xcn, awo=awo, cbe=cbe):
                s_pm.wait(v, pmneed)
                s_xc.wait(v, xcn)
                awo(v)
                cbe(v.tensor_tensor(obuf[:, oslot, :], ps_ab[bank][:, :],
                                    xc[:, xslot, :], OP.add))
            p.on("vector", evop)
            bank_slots.release_at(bank, s_ev)
            xc_slots.release_at(xslot, s_ev)
            evneed = s_ev.n
            oprev = s_od.n
            cbo = s_od.inc(16)

            def odop(g, ob=ob, tt=tt, oslot=oslot, evneed=evneed, cbo=cbo,
                     oprev=oprev):
                s_ev.wait(g, evneed)
                s_od.wait(g, oprev)
                cbo(g.dma_start(
                    out=x4T[ob * 128:(ob + 1) * 128, tt * TT:(tt + 1) * TT],
                    in_=obuf[:, oslot, :]))
            p.on("sync", odop)
            ob_slots.release_at(oslot, s_od)
        wt_slots.release_at(wslot, s_pm)
    g1_od = s_od.n
    g1_pm = s_pm.n

    p.on("gpsimd", lambda g, v=g1_od: s_od.wait(g, v))
    for tt in range(4):
        for f in range(NCH):
            i = tt * NCH + f
            xslot, xcn = xload(x4T, f, tt)
            cb = s_cast.inc(1)

            def cast_op(sc, f=f, i=i, xslot=xslot, xcn=xcn, cb=cb):
                s_xc.wait(sc, xcn)
                if i >= 2:
                    s_stmm.wait(sc, i - 1)
                sc.copy(xb[:, i % 2, :], xc[:, xslot, :])
                cb(sc.activation(xq[:, i % 2, :], xc[:, xslot, :], AF.Square))
            p.on("scalar", cast_op)
            xc_slots.release_at(xslot, s_cast)
            cb2 = s_stmm.inc(1)

            def st_mm(t, f=f, i=i, tt=tt, cb2=cb2, g1=g1_pm):
                s_cast.wait(t, i + 1)
                if f == 0 and tt == 0:
                    s_pm.wait(t, g1)
                if f == 0 and tt > 0:
                    s_row.wait(t, tt)
                t.matmul(ps_s1[:, :], ones_b[:, :], xb[:, i % 2, :],
                         start=(f == 0), stop=(f == NCH - 1))
                cb2(t.matmul(ps_s2[:, :], ones_b[:, :], xq[:, i % 2, :],
                             start=(f == 0), stop=(f == NCH - 1)))
            p.on("tensor", st_mm)
        cbv = s_vr.inc(1)

        def rows_a(v, tt=tt, cbv=cbv):
            s_stmm.wait(v, (tt + 1) * NCH)
            s_sq.wait(v, tt)
            v.tensor_scalar(mrow[:, :], ps_s1[:, :], 1.0 / D, None, OP.mult)
            v.tensor_tensor(srow[:, :], mrow[:, :], mrow[:, :], OP.mult)
            v.tensor_scalar(vrow[:, :], ps_s2[:, :], 1.0 / D, LN_EPS,
                            OP.mult, OP.add)
            cbv(v.tensor_tensor(vrow[:, :], vrow[:, :], srow[:, :],
                                OP.subtract))
        p.on("vector", rows_a)
        cbs = s_sq.inc(1)

        def rows_b(sc, tt=tt, cbs=cbs):
            s_vr.wait(sc, tt + 1)
            s_bc.wait(sc, tt)
            cbs(sc.activation(arow[:, :], vrow[:, :], AF.Sqrt))
        p.on("scalar", rows_b)
        cbr = s_row.inc(1)

        def rows_c(v, tt=tt, cbr=cbr):
            s_sq.wait(v, tt + 1)
            v.reciprocal(arow[:, :], arow[:, :])
            v.drain()
            cbr(v.scalar_tensor_tensor(brow[:, :], mrow[:, :], -1.0,
                                       arow[:, :], OP.mult, OP.mult))
        p.on("vector", rows_c)
        cbb = s_bc.inc(1)

        def bc_op(t, tt=tt, cbb=cbb):
            s_row.wait(t, tt + 1)
            if tt > 0:
                s_ap.wait(t, tt * NCH)
            t.matmul(ps_r1[:, :], ones_r[:, :], arow[:, :], start=True,
                     stop=True, skip_group_check=True)
            cbb(t.matmul(ps_r2[:, :], ones_r[:, :], brow[:, :], start=True,
                         stop=True, skip_group_check=True))
        p.on("tensor", bc_op)
        for f in range(NCH):
            xslot, xcn = xload(x4T, f, tt)
            cba = s_ap.inc(1)

            def apply_op(v, f=f, tt=tt, xslot=xslot, xcn=xcn, cba=cba):
                s_bc.wait(v, tt + 1)
                s_xc.wait(v, xcn)
                v.tensor_tensor(tmpf[:, :], xc[:, xslot, :], ps_r1[:, :],
                                OP.mult)
                cba(v.tensor_tensor(act[:, f, tt * TT:(tt + 1) * TT],
                                    tmpf[:, :], ps_r2[:, :], OP.add))
            p.on("vector", apply_op)
            xc_slots.release_at(xslot, s_ap)
    g2_ap = s_ap.n

    for qt in range(4):
        for i in range(44):
            aw = wt_slots.acquire(0)
            wprev = s_w.n
            cbw = s_w.inc(16)
            wneed = s_w.n

            def wlg(g, i=i, aw=aw, cbw=cbw, wprev=wprev):
                aw(g)
                s_w.wait(g, wprev)
                cbw(g.dma_start(out=wt[:, 0, :], in_=wgu_p[i]))
            p.on("gpsimd", wlg)
            aw2 = wtu_slots.acquire(0)
            wprev2 = s_w.n
            cbw2 = s_w.inc(16)
            wneed2 = s_w.n

            def wlu(g, i=i, aw2=aw2, cbw2=cbw2, wprev2=wprev2):
                aw2(g)
                s_w.wait(g, wprev2)
                cbw2(g.dma_start(out=wtu[:, 0, :], in_=wgu_p[44 + i]))
            p.on("gpsimd", wlu)

            bank = st["pm"] % 2
            st["pm"] += 1
            awb = bank_slots.acquire(bank)
            cbg = s_gu.inc(1)

            def gmm(t, qt=qt, bank=bank, wneed=wneed, awb=awb, cbg=cbg,
                    first=(qt == 0 and i == 0), g2=g2_ap):
                if first:
                    s_ap.wait(t, g2)
                s_w.wait(t, wneed)
                awb(t)
                for cc in range(NCH):
                    mm = t.matmul(ps_ab[bank][:, :],
                                  wt[:, 0, cc * 128:(cc + 1) * 128],
                                  act[:, cc, qt * 512:(qt + 1) * 512],
                                  start=(cc == 0), stop=(cc == NCH - 1),
                                  skip_group_check=True)
                cbg(mm)
            p.on("tensor", gmm)
            wt_slots.release_at(0, s_gu)
            gmn = s_gu.n
            awg = gs_slots.acquire(0)
            cbsg = s_gu.inc(1)

            def gev(sc, bank=bank, gmn=gmn, awg=awg, cbsg=cbsg):
                s_gu.wait(sc, gmn)
                awg(sc)
                cbsg(sc.activation(gsb[:, 0, :], ps_ab[bank][:, :], AF.Silu))
            p.on("scalar", gev)
            bank_slots.release_at(bank, s_gu)

            bank2 = st["pm"] % 2
            st["pm"] += 1
            awb2 = bank_slots.acquire(bank2)
            cbu = s_gu.inc(1)

            def umm(t, qt=qt, bank2=bank2, wneed2=wneed2, awb2=awb2, cbu=cbu):
                s_w.wait(t, wneed2)
                awb2(t)
                for cc in range(NCH):
                    mm = t.matmul(ps_ab[bank2][:, :],
                                  wtu[:, 0, cc * 128:(cc + 1) * 128],
                                  act[:, cc, qt * 512:(qt + 1) * 512],
                                  start=(cc == 0), stop=(cc == NCH - 1),
                                  skip_group_check=True)
                cbu(mm)
            p.on("tensor", umm)
            wtu_slots.release_at(0, s_gu)
            umn = s_gu.n
            awu = us_slots.acquire(0)
            cbue = s_gu.inc(1)

            def uev(v, bank2=bank2, umn=umn, awu=awu, cbue=cbue):
                s_gu.wait(v, umn)
                awu(v)
                cbue(v.tensor_copy(usb[:, 0, :], ps_ab[bank2][:, :]))
            p.on("vector", uev)
            bank_slots.release_at(bank2, s_gu)
            guns = s_gu.n
            dnn = s_dn.n
            cbhg = s_hg.inc(1)

            def hmul(v, i=i, guns=guns, dnn=dnn, cbhg=cbhg):
                s_gu.wait(v, guns)
                if dnn > 0:
                    s_dn.wait(v, dnn)
                cbhg(v.tensor_tensor(hgu[:, i, :], gsb[:, 0, :],
                                     usb[:, 0, :], OP.mult))
            p.on("vector", hmul)
            gs_slots.release_at(0, s_hg)
            us_slots.release_at(0, s_hg)
        hgn = s_hg.n

        for f in range(NCH):
            aw = wdt_slots.acquire(0)
            wprev = s_w2.n
            cbw = s_w2.inc(16)
            wneed = s_w2.n

            def wld(g, f=f, aw=aw, cbw=cbw, wprev=wprev):
                aw(g)
                s_w2.wait(g, wprev)
                cbw(g.dma_start(out=wdt[:, 0, :], in_=wdn_p[f]))
            p.on("gpsimd", wld)
            bank = st["pm2"] % 2
            st["pm2"] += 1
            awb = bank2_slots.acquire(bank)
            cbd = s_dn.inc(1)

            def dmm(t, f=f, bank=bank, wneed=wneed, awb=awb, cbd=cbd,
                    hgn=hgn):
                s_hg.wait(t, hgn)
                s_w2.wait(t, wneed)
                awb(t)
                for cc in range(44):
                    mm = t.matmul(ps_cd[bank][:, :],
                                  wdt[:, 0, cc * 128:(cc + 1) * 128],
                                  hgu[:, cc, :],
                                  start=(cc == 0), stop=(cc == 43),
                                  skip_group_check=True)
                cbd(mm)
            p.on("tensor", dmm)
            wdt_slots.release_at(0, s_dn)
            dnb = s_dn.n
            cby = s_ye.inc(1)

            def yev(v, f=f, bank=bank, dnb=dnb, cby=cby):
                s_dn.wait(v, dnb)
                cby(v.tensor_copy(yb[:, f, :], ps_cd[bank][:, :]))
            p.on("vector", yev)
            cby2 = s_ye.inc(1)

            def ysq(sc, f=f, bank=bank, yn=s_ye.n - 1, cby2=cby2,
                    i2=(qt * NCH + f)):
                s_ye.wait(sc, yn)
                if i2 >= 2:
                    s_stmm.wait(sc, 64 + i2 - 1)
                cby2(sc.activation(xq[:, i2 % 2, :], ps_cd[bank][:, :],
                                   AF.Square))
            p.on("scalar", ysq)
            cb2 = s_stmm.inc(1)

            def ystat(t, f=f, qt=qt, i2=(qt * NCH + f), cb2=cb2, yn=s_ye.n):
                s_ye.wait(t, yn)
                if f == 0 and qt > 0:
                    s_row.wait(t, 4 + qt)
                cb2(t.matmul(ps_s2[:, :], ones_b[:, :], xq[:, i2 % 2, :],
                             start=(f == 0), stop=(f == NCH - 1)))
            p.on("tensor", ystat)
            bank2_slots.release_at(bank, s_ye)

        cbv2 = s_vr.inc(1)

        def rows_v2(v, qt=qt, cbv2=cbv2, need=(64 + (qt + 1) * NCH)):
            s_stmm.wait(v, need)
            s_sq.wait(v, 4 + qt)
            cbv2(v.tensor_scalar(vrow[:, :], ps_s2[:, :], 1.0 / D, RMS_EPS,
                                 OP.mult, OP.add))
        p.on("vector", rows_v2)
        cbs2 = s_sq.inc(1)

        def rows_s2(sc, qt=qt, cbs2=cbs2, need=(4 + qt + 1)):
            s_vr.wait(sc, need)
            s_bc.wait(sc, 4 + qt)
            cbs2(sc.activation(arow[:, :], vrow[:, :], AF.Sqrt))
        p.on("scalar", rows_s2)
        cbr = s_row.inc(1)

        def rows_r2(v, qt=qt, cbr=cbr):
            s_sq.wait(v, 4 + qt + 1)
            v.reciprocal(arow[:, :], arow[:, :])
            v.drain()
            cbr(v.tensor_scalar(arow[:, :], arow[:, :], 1.0, None, OP.mult))
        p.on("vector", rows_r2)
        cbb = s_bc.inc(1)

        def bc2(t, qt=qt, cbb=cbb, fon=s_fo.n):
            s_row.wait(t, 4 + qt + 1)
            if fon > 0:
                s_fo.wait(t, fon)
            cbb(t.matmul(ps_r1[:, :], ones_r[:, :], arow[:, :], start=True,
                         stop=True, skip_group_check=True))
        p.on("tensor", bc2)

        for f in range(NCH):
            xslot, xcn = xload(x4T, f, qt)
            oslot = st["ev"] % 2
            st["ev"] += 1
            awo = ob_slots.acquire(oslot)
            cbf = s_fo.inc(1)

            def fout(v, f=f, qt=qt, xslot=xslot, xcn=xcn, oslot=oslot,
                     awo=awo, cbf=cbf):
                s_bc.wait(v, 4 + qt + 1)
                s_xc.wait(v, xcn)
                awo(v)
                v.tensor_tensor(tmpf[:, :], yb[:, f, :], ps_r1[:, :], OP.mult)
                cbf(v.tensor_tensor(obuf[:, oslot, :], tmpf[:, :],
                                    xc[:, xslot, :], OP.add))
            p.on("vector", fout)
            xc_slots.release_at(xslot, s_fo)
            fon = s_fo.n
            oprev = s_od.n
            cbo = s_od.inc(16)

            def oout(g, f=f, qt=qt, oslot=oslot, fon=fon, cbo=cbo,
                     oprev=oprev):
                s_fo.wait(g, fon)
                s_od.wait(g, oprev)
                cbo(g.dma_start(
                    out=outT[f * 128:(f + 1) * 128, qt * TT:(qt + 1) * TT],
                    in_=obuf[:, oslot, :]))
            p.on("sync", oout)
            ob_slots.release_at(oslot, s_od)

    def fin(g, v=s_od.n):
        s_od.wait(g, v)
    p.on("gpsimd", fin)
    return p.finish()


def host_inputs_G(inputs, a2_slabs, x3_slabs, wguT, wdnT):
    wout = pack16(np.ascontiguousarray(np.asarray(inputs["L1_attn_w_out"]).T))
    wgu_p = pack16(wguT)
    wdn_p = pack16(wdnT)
    in_maps = []
    for cid in range(8):
        in_maps.append({
            "a2T": a2_slabs[cid], "x3T": x3_slabs[cid],
            "wout": wout, "wgu_p": wgu_p, "wdn_p": wdn_p,
        })
    return in_maps


_MASKS_H = None
_MASKS2_H = None


def rope_tables(pos):
    inv = 1.0 / (1e6 ** (np.arange(0, 128, 2, dtype=np.float64) / 128.0))
    fr = pos[:, None].astype(np.float64) * inv[None, :]
    emb = np.concatenate([fr, fr], axis=1)
    cosT = np.cos(emb).T
    sinT = np.sin(emb).T
    sinM = sinT.copy()
    sinM[0:64] = -sinT[0:64]
    return (np.ascontiguousarray(cosT).astype(BF16NP),
            np.ascontiguousarray(sinM).astype(BF16NP))


_RSWAP = np.zeros((128, 128), np.float32)
for _i in range(128):
    _RSWAP[_i, (_i + 64) % 128] = 1.0
_RSWAP = _RSWAP.astype(BF16NP)


def host_inputs_C(inputs, a0_slabs):
    x = np.asarray(inputs["L0_attn_x"]).reshape(NTOK, D)
    wo = pack16(np.asarray(inputs["L0_attn_Wo"]))
    wq1 = pack16(np.asarray(inputs["L0_ffn_wq"]))
    wk1 = pack16(np.asarray(inputs["L0_ffn_wk"]))
    wv1 = pack_mov(np.asarray(inputs["L0_ffn_wv"]), 512)
    bq1 = np.ascontiguousarray(np.asarray(inputs["L0_ffn_bq"]).reshape(16, 128).T)
    bk1 = np.ascontiguousarray(np.asarray(inputs["L0_ffn_bk"]).reshape(16, 128).T)
    bv1 = np.asarray(inputs["L0_ffn_bv"]).reshape(1, D)
    wln1 = np.ascontiguousarray(np.asarray(inputs["L0_ffn_w_ln1"]).reshape(16, 128).T)
    in_maps = []
    for cid in range(8):
        pos = np.arange((cid % 4) * TPC, (cid % 4 + 1) * TPC)
        rc, rs = rope_tables(pos)
        in_maps.append({
            "xT": np.ascontiguousarray(x[cid * TPC:(cid + 1) * TPC].T),
            "a0t": a0_slabs[cid],
            "wo": wo, "wq1": wq1, "wk1": wk1, "wv1": wv1,
            "bq1": bq1, "bk1": bk1, "bv1": bv1, "wln1": wln1,
            "ropeC": rc, "ropeS": rs, "rswap": _RSWAP,
        })
    return in_maps


def host_inputs_E(inputs, ctx_slabs, x1_slabs, xh1_slabs):
    wo2 = pack16(np.asarray(inputs["L0_ffn_wo"]))
    wln2 = np.ascontiguousarray(np.asarray(inputs["L0_ffn_w_ln2"]).reshape(16, 128).T)
    _w9 = np.concatenate(
        [np.asarray(inputs["L0_ffn_w_router"]),
         np.asarray(inputs["L0_ffn_w_sh_eg"]),
         np.zeros((D, 7), np.float32)], axis=1)
    wrt9 = np.ascontiguousarray(_w9.reshape(16, 128, 16))
    wg = np.stack([pack16(np.asarray(inputs["L0_ffn_w_gate_e"][e])) for e in range(8)])
    wu = np.stack([pack16(np.asarray(inputs["L0_ffn_w_up_e"][e])) for e in range(8)])
    wd = np.stack([pack16(np.asarray(inputs["L0_ffn_w_down_e"][e])) for e in range(8)])
    wsg = pack16(np.asarray(inputs["L0_ffn_w_sh_gate"]))
    wsu = pack16(np.asarray(inputs["L0_ffn_w_sh_up"]))
    wsd = pack16(np.asarray(inputs["L0_ffn_w_sh_down"]))
    wpT = np.ascontiguousarray(np.asarray(inputs["L1_attn_w_project"]).T)
    wpq = pack16(wpT[:, 0:4096])
    wpv = pack_mov(np.ascontiguousarray(wpT[:, 4096:6144]), 256)
    idf = np.eye(128, dtype=np.float32)
    in_maps = []
    for cid in range(8):
        in_maps.append({
            "ctxT": ctx_slabs[cid], "x1T": x1_slabs[cid], "xh1T": xh1_slabs[cid],
            "wo2": wo2, "wln2": wln2, "wrt9": wrt9, "wg": wg, "wu": wu,
            "wd": wd, "wsg": wsg, "wsu": wsu, "wsd": wsd, "wpq": wpq,
            "wpv": wpv, "idf": idf, "idb": _IDENT,
        })
    return in_maps


def host_attn_inputs(q0t_cores, k0t_cores, v0_cores):
    global _MASKS_H, _MASKS2_H
    if _MASKS_H is None:
        m = causal_mask_tiles()
        _MASKS_H = np.ascontiguousarray(m.transpose(1, 0, 2).reshape(128, 4 * 512))
        pcol = np.arange(128)[:, None]
        ccol = np.arange(512)[None, :]
        m2 = np.stack([np.where(ccol <= 128 * r + pcol, 0.0, NEG)
                       for r in range(4)], axis=1)
        _MASKS2_H = np.ascontiguousarray(
            m2.reshape(128, 4 * 512).astype(np.float32))
    in_maps = []
    for c2 in range(8):
        qT = np.empty((4, 128, S), BF16NP)
        kT = np.empty((4, 128, S), BF16NP)
        vb = np.empty((4, 128, S), BF16NP)
        for pp in range(4):
            gidx = c2 * 4 + pp
            b, h = divmod(gidx, HEADS)
            for j in range(4):
                src = q0t_cores[b * 4 + j]
                qT[pp, :, j * TPC:(j + 1) * TPC] = src[h * 128:(h + 1) * 128, :]
                kT[pp, :, j * TPC:(j + 1) * TPC] = \
                    k0t_cores[b * 4 + j][h * 128:(h + 1) * 128, :]
            vp = np.concatenate(
                [v0_cores[b * 4 + j][:, h * 128:(h + 1) * 128] for j in range(4)],
                axis=0)
            vb[pp] = vp.reshape(S // 128, 128, 128).transpose(1, 0, 2).reshape(128, S)
        in_maps.append({"qT": qT, "kT": kT, "vblk": vb, "masks": _MASKS_H,
                        "masks2": _MASKS2_H, "idb": _IDENT})
    return in_maps


def host_attn_out_to_token_slabs(res):
    slabs = [np.empty((D, TPC), BF16NP) for _ in range(8)]
    for c2 in range(8):
        o = res[c2]["outT"]
        for pp in range(4):
            gidx = c2 * 4 + pp
            b, h = divmod(gidx, HEADS)
            for j in range(4):
                slabs[b * 4 + j][h * 128:(h + 1) * 128, :] = \
                    o[pp][:, j * TPC:(j + 1) * TPC]
    return slabs



_IDENT = np.eye(128, dtype=BF16NP)


def host_inputs_A(inputs):
    x = np.asarray(inputs["L0_attn_x"]).reshape(NTOK, D)
    wq = pack16(np.asarray(inputs["L0_attn_Wq"]))
    wdown = pack16(np.asarray(inputs["L0_attn_Wdown"]))
    wkup = pack16(np.asarray(inputs["L0_attn_Wk_up"]))
    wvup = pack_mov(np.asarray(inputs["L0_attn_Wv_up"]), 512)
    gup = np.asarray(inputs["L1_ffn_gate_up_packed"]).reshape(88, 128, 256)
    gusT = np.ascontiguousarray(
        np.asarray(inputs["L1_ffn_gate_up_scales"]).T).reshape(88, 128, 16)
    dnp = np.asarray(inputs["L1_ffn_down_packed"]).reshape(16, 128, 704)
    dnsT = np.ascontiguousarray(
        np.asarray(inputs["L1_ffn_down_scales"]).T).reshape(16, 128, 44)
    in_maps = []
    for cid in range(8):
        in_maps.append({
            "xT": np.ascontiguousarray(x[cid * TPC:(cid + 1) * TPC].T),
            "wq": wq, "wdown": wdown, "wkup": wkup, "wvup": wvup,
            "gup": np.ascontiguousarray(gup[cid * 11:(cid + 1) * 11]),
            "gusT": np.ascontiguousarray(gusT[cid * 11:(cid + 1) * 11]),
            "dnp": np.ascontiguousarray(dnp[cid * 2:(cid + 1) * 2]),
            "dnsT": np.ascontiguousarray(dnsT[cid * 2:(cid + 1) * 2]),
            "ident": _IDENT,
        })
    return in_maps


def run_A(inputs):
    nc = build_launch_A()
    return run_bass_kernel_spmd(nc, host_inputs_A(inputs), core_ids=CORES).results


def kernel(**inputs):
    resA = run_A(inputs)
    ncB = build_launch_attn(dbg=("nostats",))
    resB = run_bass_kernel_spmd(ncB, host_attn_inputs(
        [resA[c]["q0t"] for c in range(8)],
        [resA[c]["k0t"] for c in range(8)],
        [resA[c]["v0"] for c in range(8)]), core_ids=CORES).results
    a0 = host_attn_out_to_token_slabs(resB)
    ncC = build_launch_C()
    resC = run_bass_kernel_spmd(ncC, host_inputs_C(inputs, a0),
                                core_ids=CORES).results
    resD = run_bass_kernel_spmd(ncB, host_attn_inputs(
        [resC[c]["q1t"] for c in range(8)],
        [resC[c]["k1t"] for c in range(8)],
        [resC[c]["v1"] for c in range(8)]), core_ids=CORES).results
    ctx1 = host_attn_out_to_token_slabs(resD)
    ncE = build_launch_E()
    resE = run_bass_kernel_spmd(ncE, host_inputs_E(
        inputs, ctx1, [resC[c]["x1T"] for c in range(8)],
        [resC[c]["xh1T"] for c in range(8)]), core_ids=CORES).results
    resF = run_bass_kernel_spmd(ncB, host_attn_inputs(
        [resE[c]["q2t"] for c in range(8)],
        [resE[c]["k2t"] for c in range(8)],
        [resE[c]["v2"] for c in range(8)]), core_ids=CORES).results
    a2 = host_attn_out_to_token_slabs(resF)
    wguT = np.concatenate([resA[c]["wguTs"] for c in range(8)], axis=1)
    wdnT = np.concatenate([resA[c]["wdnTs"] for c in range(8)], axis=1)
    ncG = build_launch_G()
    resG = run_bass_kernel_spmd(ncG, host_inputs_G(
        inputs, a2, [resE[c]["x3T"] for c in range(8)], wguT, wdnT),
        core_ids=CORES).results
    out = np.empty((NTOK, D), np.float32)
    for cid in range(8):
        out[cid * TPC:(cid + 1) * TPC] = resG[cid]["outT"].T
    return out.reshape(B, S, D)
